# version 26
# baseline (speedup 1.0000x reference)
import numpy as np
import ml_dtypes
import concourse.bass as bass
import concourse.mybir as mybir
from concourse.bass_utils import run_bass_kernel_spmd

F32 = mybir.dt.float32
BF16 = mybir.dt.bfloat16
I32 = mybir.dt.int32
U8 = mybir.dt.uint8
AF = mybir.ActivationFunctionType
ALU = mybir.AluOpType
AX = mybir.AxisListType

D = 2048
HID = 5632
NH = 16
T = 2048
CH = 1024
NCH = 2
KCN = 16
HCN = 44
NGRP = CH // 8
NBAT = NGRP // 4
PERM = [(h % 4) * 4 + h // 4 for h in range(NH)]
RSQ = 0.08838834764831845
EPS = 1e-5
SHIFT = -8.0

ENGS = ("sync", "gpsimd", "tensor", "vector", "scalar")


class Plan:
    def __init__(self):
        self.ops = {e: [] for e in ENGS}
        self.cnt = {}

    def sem(self, name):
        self.cnt.setdefault(name, 0)

    def add(self, eng, fn, waits=(), inc=None, amt=1):
        w = [(s, c) for (s, c) in waits if s is not None and c > 0]
        for s, c in w:
            assert self.cnt.get(s, 0) >= c, f"wait on future token {s}={c} (have {self.cnt.get(s, 0)})"
        tok = None
        if inc is not None:
            self.cnt[inc] = self.cnt.get(inc, 0) + amt
            tok = (inc, self.cnt[inc])
        self.ops[eng].append((w, fn, inc, amt))
        return tok

    def emit(self, block, sems):
        def mk(name):
            def body(eng):
                have = {}
                for w, fn, inc, amt in self.ops[name]:
                    for s, c in w:
                        if have.get(s, 0) < c:
                            eng.wait_ge(sems[s], c)
                            have[s] = c
                    ins = fn(eng)
                    if inc is not None:
                        ins.then_inc(sems[inc], amt)
            return body
        block.sync(mk("sync"))
        block.gpsimd(mk("gpsimd"))
        block.tensor(mk("tensor"))
        block.vector(mk("vector"))
        block.scalar(mk("scalar"))


def mx(*toks):
    best = {}
    for t in toks:
        if t is None:
            continue
        for it in (t if isinstance(t, list) else [t]):
            if it is None:
                continue
            s, c = it
            if best.get(s, 0) < c:
                best[s] = c
    return list(best.items())


def build_program():
    nc = bass.Bass("TRN2", target_bir_lowering=False)

    dx = nc.dram_tensor("x", [T, D], F32, kind="ExternalInput").ap()
    dwq = nc.dram_tensor("wq", [8, 128, 4096], BF16, kind="ExternalInput").ap()
    dwk = nc.dram_tensor("wk", [8, 128, 4096], BF16, kind="ExternalInput").ap()
    dwv = nc.dram_tensor("wv", [8, 128, 4096], BF16, kind="ExternalInput").ap()
    dwo = nc.dram_tensor("wo", [8, 128, 4096], BF16, kind="ExternalInput").ap()
    dwgu = nc.dram_tensor("wgu", [HCN, 128, 4096], U8, kind="ExternalInput").ap()
    dwd = nc.dram_tensor("wd", [16, 128, HID], U8, kind="ExternalInput").ap()
    dsg = nc.dram_tensor("sg", [HCN, 128], F32, kind="ExternalInput").ap()
    dsu = nc.dram_tensor("su", [HCN, 128], F32, kind="ExternalInput").ap()
    dsd = nc.dram_tensor("sd", [16, 128], F32, kind="ExternalInput").ap()
    deye = nc.dram_tensor("eye", [128, 128], F32, kind="ExternalInput").ap()
    dmska = nc.dram_tensor("mska", [9, 128], BF16, kind="ExternalInput").ap()
    dmskb = nc.dram_tensor("mskb", [9, 512], BF16, kind="ExternalInput").ap()
    dy = nc.dram_tensor("y", [T, D], F32, kind="ExternalOutput").ap()

    BASE = 17408

    def at(name, shape, dt, o):
        return nc.alloc_sbuf_tensor_at(name, shape, dt, offset=BASE + o)

    xnT = at("xnT", [128, 16384], BF16, 0)
    out_nat = at("out_nat", [128, 8192], F32, 0)
    Qb = at("Qb", [128, 16384], BF16, 32768)
    Kb = at("Kb", [128, 16384], BF16, 65536)
    Vb = at("Vb", [128, 16384], BF16, 98304)
    actT = at("actT", [128, 45056], BF16, 32768)
    xstg0 = at("xstg0", [128, 2048], F32, 32768)
    xstg1 = at("xstg1", [128, 2048], F32, 40960)
    xn_bs = [at(f"xn_b{k}", [128, 2048], BF16, 49152 + 4096 * k) for k in range(4)]
    attnT = at("attnT", [128, 16384], BF16, 131072)
    A0 = 163840
    xbuf0 = at("xbuf0", [128, 2048], F32, A0)
    xbuf1 = at("xbuf1", [128, 2048], F32, A0 + 8192)
    e_f0 = at("e_f0", [128, 512], F32, A0)
    e_f1 = at("e_f1", [128, 512], F32, A0 + 2048)
    e_f2 = at("e_f2", [128, 512], F32, A0 + 4096)
    p_b0 = at("p_b0", [128, 512], BF16, A0 + 6144)
    p_b1 = at("p_b1", [128, 512], BF16, A0 + 7168)
    p_b2 = at("p_b2", [128, 512], BF16, A0 + 8192)
    pT_s0 = at("pT_s0", [128, 512], BF16, A0 + 9216)
    pT_s1 = at("pT_s1", [128, 512], BF16, A0 + 10240)
    pT_s2 = at("pT_s2", [128, 512], BF16, A0 + 11264)
    vT_s0 = at("vT_s0", [128, 512], BF16, A0 + 12288)
    vT_s1 = at("vT_s1", [128, 512], BF16, A0 + 13312)
    vT_s2 = at("vT_s2", [128, 512], BF16, A0 + 14336)
    gate_b = at("gate_b", [128, 1024], BF16, A0)
    up_b = at("up_b", [128, 1024], BF16, A0 + 2048)
    dwst0 = at("dwst0", [128, 2816], BF16, A0)
    dwst1 = at("dwst1", [128, 2816], BF16, A0 + 5632)
    outT0 = at("outT0", [128, 512], BF16, A0 + 14336)
    outT1 = at("outT1", [128, 512], BF16, A0 + 15360)
    wst0 = at("wst0", [128, 4096], BF16, 180224)
    wst1 = at("wst1", [128, 4096], BF16, 188416)
    m = 196608

    def ata(name, shape, dt, nbytes):
        nonlocal m
        m = (m + 63) // 64 * 64
        t = at(name, shape, dt, m)
        m += nbytes
        return t
    ata("pad0", [1, 32], F32, 128)
    eye_f = ata("eye_f", [128, 128], F32, 512)
    eye_b = ata("eye_b", [128, 128], BF16, 256)
    mkA = ata("mkA", [9, 128], BF16, 256)
    mkB = ata("mkB", [9, 512], BF16, 1024)
    shift_t = ata("shift_t", [128, 1], F32, 16)
    sgT = ata("sgT", [128, 44], F32, 192)
    suT = ata("suT", [128, 44], F32, 192)
    sdT = ata("sdT", [128, 16], F32, 64)
    eps_t = ata("eps_t", [128, 1], F32, 16)
    ones_c = ata("ones_c", [128, 1], BF16, 16)
    cneg = ata("cneg", [2, 128], BF16, 256)
    bn6 = ata("bn6", [128, 96], F32, 384)
    mv2 = ata("mv2", [128, 8], F32, 32)
    stdt = ata("stdt", [128, 4], F32, 16)
    rstd = ata("rstd", [128, 4], F32, 16)
    nmr = ata("nmr", [128, 4], F32, 16)
    rs4 = ata("rs4", [128, 12], F32, 48)
    ri4 = ata("ri4", [128, 12], F32, 48)
    hilo_xn = ata("hilo_xn", [2, 1024], BF16, 2048)
    hilo_ac = hilo_xn
    lo_b = ata("lo_b", [1, 512], BF16, 1024)
    scl_stage = ata("scl_stage", [44, 384], F32, 1536)
    corr_sb = [ata("corr0", [128, 512], F32, 2048), ata("corr1", [128, 512], F32, 2048)]
    assert BASE + m <= 229376 - 512, f"sbuf over budget: {BASE + m}"

    p0 = nc.alloc_psum_tensor("p0", [128, 512], F32)
    p1 = nc.alloc_psum_tensor("p1", [128, 512], F32)
    p2 = nc.alloc_psum_tensor("p2", [128, 512], F32)
    p3 = nc.alloc_psum_tensor("p3", [128, 512], F32)
    pb4 = nc.alloc_psum_tensor("pb4", [128, 1024], BF16)
    pb5 = nc.alloc_psum_tensor("pb5", [128, 1024], BF16)
    pb6 = nc.alloc_psum_tensor("pb6", [128, 1024], BF16)
    pb7 = nc.alloc_psum_tensor("pb7", [128, 1024], BF16)
    pf4 = pb4[:].bitcast(F32)
    pf5 = pb5[:].bitcast(F32)
    pf6 = pb6[:].bitcast(F32)
    pf7 = pb7[:].bitcast(F32)
    PBF = [p0, p1, p2, p3]
    ALLP = [p0, p1, p2, p3, pb4, pb5, pb6, pb7]

    P = Plan()
    for s in ("SV", "SS", "ST", "dxa", "dxb", "dsc", "dhl", "dn", "dout",
              "wa", "wb", "da", "db", "ca", "cb", "cc", "cd", "dsl", "dmk",
              "dxe", "dxf", "cg", "ch"):
        P.sem(s)

    bank_free = {id(b): None for b in ALLP}
    WST = [wst0, wst1]
    wslot_free = [None, None]
    wslot_sem = ["wa", "wb"]
    wct = {"i": 0}
    DWST = [dwst0, dwst1]
    dslot_free = [None, None]
    dslot_sem = ["da", "db"]
    dct = {"i": 0}
    XBUF = [xbuf0, xbuf1, xstg0, xstg1]
    xslot_free = [None, None, None, None]
    xslot_sem = ["dxa", "dxb", "ca", "cb"]
    xslot_sem2 = ["dxe", "dxf", "cg", "ch"]

    st_ = {
        "attn_pe_done": None,
        "attn_scr_free": None,
        "attnT_read": None,
        "actT_read": None,
        "out_store": None,
        "xnT_free": None,
        "ln_xbuf_done": None,
        "outT_free": [None, None],
        "corr_free": [None, None],
        "actT_read_early": None,
    }

    P.add("gpsimd", lambda e: e.dma_start(eye_f[:], deye), inc="dsc", amt=16)
    tok_eye = ("dsc", P.cnt["dsc"])
    P.add("gpsimd", lambda e: e.dma_start(scl_stage[0:44, 0:128], dsg), inc="dsl", amt=16)
    P.add("gpsimd", lambda e: e.dma_start(scl_stage[0:44, 128:256], dsu), inc="dsl", amt=16)
    P.add("gpsimd", lambda e: e.dma_start(scl_stage[0:16, 256:384], dsd), inc="dsl", amt=16)
    tok_scl = ("dsl", P.cnt["dsl"])
    P.add("gpsimd", lambda e: e.dma_start(mkA[:], dmska), inc="dmk", amt=16)
    P.add("gpsimd", lambda e: e.dma_start(mkB[:], dmskb), inc="dmk", amt=16)
    tok_mk = ("dmk", P.cnt["dmk"])
    tok_eyeb = P.add("vector", lambda e: e.tensor_copy(eye_b[:], eye_f[:]),
                     waits=mx(tok_eye), inc="SV")
    tok_eps = P.add("vector", lambda e: e.memset(eps_t[:], EPS), inc="SV")
    tok_ones = P.add("vector", lambda e: e.memset(ones_c[:], 1.0), inc="SV")
    tok_cneg = P.add("vector", lambda e: e.memset(cneg[:], -128.0), inc="SV")
    tok_shift = P.add("vector", lambda e: e.memset(shift_t[:], SHIFT), inc="SV")
    tk = P.add("tensor", lambda e: e.transpose(pf4[0:128, 0:44], scl_stage[0:44, 0:128], eye_f[0:44, 0:44]),
               waits=mx(tok_eye, tok_scl), inc="ST")
    tok_sg = P.add("scalar", lambda e: e.activation(sgT[:], pf4[0:128, 0:44], AF.Copy),
                   waits=mx(tk), inc="SS")
    tk = P.add("tensor", lambda e: e.transpose(pf4[0:128, 44:88], scl_stage[0:44, 128:256], eye_f[0:44, 0:44]),
               waits=mx(tok_sg), inc="ST")
    tok_su = P.add("scalar", lambda e: e.activation(suT[:], pf4[0:128, 44:88], AF.Copy),
                   waits=mx(tk), inc="SS")
    tk = P.add("tensor", lambda e: e.transpose(pf4[0:128, 88:104], scl_stage[0:16, 256:384], eye_f[0:16, 0:16]),
               waits=mx(tok_su), inc="ST")
    tok_sd = P.add("scalar", lambda e: e.activation(sdT[:], pf4[0:128, 88:104], AF.Copy),
                   waits=mx(tk), inc="SS")
    bank_free[id(pb4)] = tok_sd

    def stream_w(src_ap, waits_extra=()):
        i = wct["i"] % 2
        wct["i"] += 1
        dst = WST[i][:, 0:src_ap.shape[-1]]
        tokd = P.add("gpsimd", lambda e, d=dst, s=src_ap: e.dma_start(d, s),
                     waits=mx(wslot_free[i], *waits_extra), inc=wslot_sem[i], amt=16)
        return i, tokd

    def stream_d(src_ap, waits_extra=()):
        i = dct["i"] % 2
        dct["i"] += 1
        dst = DWST[i][:, 0:src_ap.shape[-1]]
        tokd = P.add("gpsimd", lambda e, d=dst, s=src_ap: e.dma_start(d, s),
                     waits=mx(dslot_free[i], *waits_extra), inc=dslot_sem[i], amt=16)
        return i, tokd

    def ln_phase(c):
        xnT_w = []
        prev_norm = [None, None, None, None]
        prev_aggr_rd = [None, None, None, None]
        prev_tr_last = [None, None, None, None]
        toks_x = [None] * 8

        def emit_x(i):
            xs = i % 4
            rows = c * CH + i * 128
            if xs < 2:
                w = mx(xslot_free[xs],
                       (dslot_free[0] if c > 0 else None),
                       (dslot_free[1] if c > 0 else None),
                       (st_["outT_free"][0] if c > 0 else None),
                       (st_["outT_free"][1] if c > 0 else None))
            else:
                w = mx(xslot_free[xs],
                       (st_["attn_pe_done"] if c > 0 else None),
                       (st_["actT_read_early"] if c > 0 else None))
            if i < 2:
                for q0, q1 in ((0, 32), (64, 96)):
                    P.add("sync",
                          lambda e, d=XBUF[xs][q0:q0 + 32, :], s=dx[rows + q0:rows + q0 + 32, :]:
                          e.dma_start(d, s), waits=w, inc=xslot_sem[xs], amt=16)
                    P.add("scalar",
                          lambda e, d=XBUF[xs][q1:q1 + 32, :], s=dx[rows + q1:rows + q1 + 32, :]:
                          e.dma_start(d, s), waits=w, inc=xslot_sem2[xs], amt=16)
                ta = (xslot_sem[xs], P.cnt[xslot_sem[xs]])
                tb = (xslot_sem2[xs], P.cnt[xslot_sem2[xs]])
            else:
                ta = P.add("sync",
                           lambda e, d=XBUF[xs][0:64, :], s=dx[rows:rows + 64, :]:
                           e.dma_start(d, s), waits=w, inc=xslot_sem[xs], amt=16)
                tb = P.add("scalar",
                           lambda e, d=XBUF[xs][64:128, :], s=dx[rows + 64:rows + 128, :]:
                           e.dma_start(d, s), waits=w, inc=xslot_sem2[xs], amt=16)
            toks_x[i] = (ta, tb)

        tnorm_all = [None] * 8
        PBQ = [pb4, pb5, pb6, pb7]

        def stage1(i):
            sl = i % 4
            xs = i % 4
            tb4 = None
            for j in range(4):
                tb4 = P.add("vector",
                            lambda e, o=bn6[:, 24 * sl + 6 * j:24 * sl + 6 * j + 6],
                                   s=XBUF[xs][:, 512 * j:512 * (j + 1)]: e.bn_stats(o, s),
                            waits=mx(toks_x[i][0], toks_x[i][1], prev_aggr_rd[sl]), inc="SV")
            tagg = P.add("vector",
                         lambda e, o=mv2[:, 2 * sl:2 * sl + 2], s=bn6[:, 24 * sl:24 * sl + 24]:
                         e.bn_aggr(o, s),
                         waits=mx(tb4, prev_norm[sl]), inc="SV")
            tsq = P.add("scalar",
                        lambda e, o=stdt[:, sl:sl + 1], s=mv2[:, 2 * sl + 1:2 * sl + 2]:
                        e.activation(o, s, AF.Sqrt, bias=eps_t[:]),
                        waits=mx(tagg, tok_eps, prev_norm[sl]), inc="SS")
            trc = P.add("vector",
                        lambda e, o=rstd[:, sl:sl + 1], s=stdt[:, sl:sl + 1]: e.reciprocal(o, s),
                        waits=mx(tsq, prev_norm[sl]), inc="SV")
            tnm = P.add("vector",
                        lambda e, o=nmr[:, sl:sl + 1], s=mv2[:, 2 * sl:2 * sl + 1],
                               r=rstd[:, sl:sl + 1]:
                        e.tensor_scalar(o, s, r, -1.0, ALU.mult, ALU.mult),
                        waits=mx(trc, prev_norm[sl]), inc="SV")
            xnb = xn_bs[sl]
            wnorm = mx(tagg, trc, tnm, prev_tr_last[sl],
                       st_["attn_pe_done"] if c > 0 else None,
                       st_["actT_read_early"] if c > 0 else None)
            tnorm_h = []
            for hh in range(2):
                tnorm = P.add("scalar",
                              lambda e, o=xnb[:, hh * 1024:(hh + 1) * 1024],
                                     s=XBUF[xs][:, hh * 1024:(hh + 1) * 1024],
                                     b=nmr[:, sl:sl + 1], r=rstd[:, sl:sl + 1]:
                              e.activation(o, s, AF.Identity, bias=b, scale=r),
                              waits=wnorm, inc="SS")
                tnorm_h.append(tnorm)
            tnorm_all[i] = tnorm_h
            prev_norm[sl] = tnorm_h[1]
            prev_aggr_rd[sl] = tnorm_h[1]
            xslot_free[xs] = tnorm_h[1]
            if i + 4 < 8:
                emit_x(i + 4)

        def stage2(i):
            sl = i % 4
            xnb = xn_bs[sl]
            tnorm_h = tnorm_all[i]
            last_tr = None
            for h in range(2):
                pb = PBQ[(i * 2 + h) % 4]
                t_tr = None
                for j in range(8):
                    kc = h * 8 + j
                    t_tr = P.add("tensor",
                                 lambda e, o=pb[:, j * 128:(j + 1) * 128],
                                        s=xnb[:, kc * 128:(kc + 1) * 128]:
                                 e.transpose(o, s, eye_b[:]),
                                 waits=mx(tnorm_h[h], tok_eyeb, bank_free[id(pb)]) if j == 0 else [],
                                 inc="ST" if j == 7 else None)
                dst = xnT[:].rearrange("p (k t) -> p k t", t=1024)[:, h * 8:(h + 1) * 8,
                                                                  i * 128:(i + 1) * 128]
                src = pb[:, 0:1024].rearrange("p (k t) -> p k t", t=128)
                wv = mx(t_tr, st_["out_store"] if c > 0 else None)
                if h == 0:
                    ev = P.add("scalar", lambda e, o=dst, s=src: e.activation(o, s, AF.Copy),
                               waits=wv, inc="SS")
                else:
                    ev = P.add("vector", lambda e, o=dst, s=src: e.tensor_copy(o, s),
                               waits=wv, inc="SV")
                bank_free[id(pb)] = ev
                xnT_w.append((i, ev))
                last_tr = t_tr
            prev_tr_last[sl] = last_tr

        for i in range(4):
            emit_x(i)
        for i in range(8):
            stage1(i)
            if i >= 1:
                stage2(i - 1)
        stage2(7)
        st_["ln_xbuf_done"] = mx(prev_norm[0], prev_norm[1])
        r0 = mx(*[t for (i, t) in xnT_w if i < 4])
        r1 = mx(*[t for (i, t) in xnT_w])
        return (r0, r1)

    def colsum(ready, dst_hilo, mov_fn, nacc, pbank, war):
        toks = []
        for tb in range(2):
            wt = mx(ready, bank_free[id(pbank)], tok_ones, war)
            t_mm = None
            for k in range(nacc):
                t_mm = P.add("tensor",
                             lambda e, o=pbank[0:1, 0:512], s=ones_c[:], r=mov_fn(k, tb),
                                    stt=(k == 0), sp=(k == nacc - 1):
                             e.matmul(o, s, r, start=stt, stop=sp),
                             waits=wt if k == 0 else [], inc="ST" if k == nacc - 1 else None)
            th = P.add("vector",
                       lambda e, o=dst_hilo[0:1, tb * 512:(tb + 1) * 512], s=pbank[0:1, 0:512]:
                       e.tensor_copy(o, s), waits=mx(t_mm, war, [("dhl", P.cnt["dhl"])]), inc="SV")
            tl = P.add("vector",
                       lambda e, o=lo_b[0:1, :], s0=pbank[0:1, 0:512],
                              s1=dst_hilo[0:1, tb * 512:(tb + 1) * 512]:
                       e.tensor_sub(o, s0, s1), waits=mx(th), inc="SV")
            bank_free[id(pbank)] = tl
            td = P.add("sync",
                       lambda e, o=dst_hilo[1:2, tb * 512:(tb + 1) * 512], s=lo_b[0:1, :]:
                       e.dma_start(o, s), waits=mx(tl), inc="dhl", amt=16)
            toks.append(td)
        return toks

    def qkv_phase(c, ready):
        evac_toks = []
        for dw, dstb, qperm in ((dwq, Qb, True), (dwk, Kb, False), (dwv, Vb, False)):
            for pr in range(8):
                sl, tokd = stream_w(dw[pr])
                for b in range(2):
                    ob = pr * 2 + b
                    for tb in range(2):
                        bank = PBF[b * 2 + tb]
                        wt = mx(tokd, bank_free[id(bank)], ready[tb])
                        t_mm = None
                        for kc in range(KCN):
                            t_mm = P.add("tensor",
                                         lambda e, o=bank[:],
                                                s=WST[sl][:, b * 2048 + kc * 128: b * 2048 + (kc + 1) * 128],
                                                r=xnT[:, kc * 1024 + tb * 512: kc * 1024 + tb * 512 + 512],
                                                stt=(kc == 0), sp=(kc == KCN - 1):
                                         e.matmul(o, s, r, start=stt, stop=sp),
                                         waits=wt if kc == 0 else [], inc="ST" if kc == KCN - 1 else None)
                        wslot_free[sl] = t_mm
                        slot_h = PERM[ob] if qperm else ob
                        dview = dstb[:].rearrange("p (g s i) -> p g s i", s=NH, i=8)
                        dst = dview[:, tb * 64:(tb + 1) * 64, slot_h, :]
                        src = bank[:].rearrange("p (g i) -> p g i", i=8)
                        wv = mx(t_mm,
                                st_["attn_pe_done"] if c > 0 else None,
                                st_["actT_read"] if c > 0 else None)
                        if (ob + tb) % 2 == 0:
                            ev = P.add("scalar", lambda e, o=dst, s=src: e.activation(o, s, AF.Copy),
                                       waits=wv, inc="SS")
                        else:
                            ev = P.add("vector", lambda e, o=dst, s=src: e.tensor_copy(o, s),
                                       waits=wv, inc="SV")
                        bank_free[id(bank)] = ev
                        evac_toks.append(ev)
        return mx(*evac_toks)

    def attn_phase(c, qkv_done):
        EF = [e_f0, e_f1, e_f2]
        PBM = [p_b0, p_b1, p_b2]
        PTS = [pT_s0, pT_s1, pT_s2]
        VTS = [vT_s0, vT_s1, vT_s2]
        PSC = [p0, p1]
        PAV = [p2, p3]
        prev = {}
        ev_toks = []
        state = {"last_pe": None, "t_e_last": None, "t_mu_last": None}
        pendA = []
        pendB = []

        def do_tailA(bb, t_pm_toks):
            par = bb % 2
            par3 = bb % 3
            pPT = pb4 if par == 0 else pb5
            t_tr = None
            for j in range(4):
                t_tr = P.add("tensor",
                             lambda e, o=pPT[:, j * 128:(j + 1) * 128],
                                    s=PBM[par3][:, j * 128:(j + 1) * 128]:
                             e.transpose(o, s, eye_b[:]),
                             waits=mx(t_pm_toks[j],
                                      bank_free[id(pPT)] if j == 0 else None,
                                      tok_eyeb if j == 0 else None),
                             inc="ST" if j == 3 else None)
            prev[("pbm", par3)] = t_tr
            t_pte = P.add("scalar", lambda e, o=PTS[par3][:], s=pPT[:, 0:512]: e.activation(o, s, AF.Copy),
                          waits=mx(t_tr, prev.get(("pts", par3))), inc="SS")
            bank_free[id(pPT)] = t_pte
            pendB.append((bb, t_pte))

        def do_tailB(bb, t_pte):
            par3 = bb % 3
            pAV = PAV[bb % 2]
            t_av = None
            for j in range(4):
                t_av = P.add("tensor",
                             lambda e, o=pAV[:, j * 128:(j + 1) * 128],
                                    s=VTS[par3][:, j * 128:(j + 1) * 128],
                                    r=PTS[par3][:, j * 128:(j + 1) * 128]:
                             e.matmul(o, s, r, start=True, stop=True),
                             waits=mx(prev.get(("vte", par3)), t_pte, bank_free[id(pAV)]) if j == 0 else [],
                             inc="ST" if j == 3 else None)
            prev[("vts", par3)] = t_av
            prev[("pts", par3)] = t_av
            state["last_pe"] = t_av
            dst = attnT[:].rearrange("p (h gg i) -> p h gg i", gg=NGRP, i=8)[:, :, bb * 4:(bb + 1) * 4, :]
            src = pAV[:].rearrange("p (g h i) -> p h g i", g=4, i=8)
            t_ev = P.add("scalar", lambda e, o=dst, s=src: e.activation(o, s, AF.Copy),
                         waits=mx(t_av, prev.get(("avev",)),
                                  st_["attnT_read"] if c > 0 else None), inc="SS")
            prev[("avev",)] = t_ev
            bank_free[id(pAV)] = t_ev
            ev_toks.append(t_ev)

        for b in range(NBAT):
            par = b % 2
            par3 = b % 3
            pS = PSC[b % 2]
            pVT = pb6 if par == 0 else pb7
            for j in range(4):
                g = b * 4 + j
                P.add("tensor",
                      lambda e, o=pS[:, j * 128:(j + 1) * 128],
                             s=Qb[:, g * 128:(g + 1) * 128], r=Kb[:, g * 128:(g + 1) * 128],
                             stt=(j == 0):
                      e.matmul(o, s, r, start=stt, stop=False),
                      waits=mx(qkv_done, bank_free[id(pS)]) if j == 0 else [],
                      inc=None)
            t_sc = P.add("tensor",
                         lambda e, o=pS[:], s=mkA[:], r=mkB[:]:
                         e.matmul(o, s, r, start=False, stop=True),
                         waits=mx(tok_mk), inc="ST")
            t_vt = None
            for j in range(4):
                g = b * 4 + j
                t_vt = P.add("tensor",
                             lambda e, o=pVT[:, j * 128:(j + 1) * 128], s=Vb[:, g * 128:(g + 1) * 128]:
                             e.transpose(o, s, eye_b[:]),
                             waits=mx(qkv_done, bank_free[id(pVT)], tok_eyeb) if j == 0 else [],
                             inc="ST" if j == 3 else None)
            if len(pendA) == 2:
                do_tailA(*pendA.pop(0))
            if len(pendB) == 2:
                do_tailB(*pendB.pop(0))
            if b % 2 == 0:
                t_vte = P.add("vector", lambda e, o=VTS[par3][:], s=pVT[:, 0:512]: e.tensor_copy(o, s),
                              waits=mx(t_vt, prev.get(("vts", par3))), inc="SV")
            else:
                t_vte = P.add("scalar", lambda e, o=VTS[par3][:], s=pVT[:, 0:512]: e.activation(o, s, AF.Copy),
                              waits=mx(t_vt, prev.get(("vts", par3))), inc="SS")
            prev[("vte", par3)] = t_vte
            bank_free[id(pVT)] = t_vte
            t_e = P.add("scalar",
                        lambda e, o=EF[par3][:], s=pS[:]:
                        e.activation(o, s, AF.Exp, bias=shift_t[:], scale=RSQ),
                        waits=mx(t_sc, tok_shift, prev.get(("ef", par3)),
                                 st_["ln_xbuf_done"]), inc="SS")
            bank_free[id(pS)] = t_e
            state["t_e_last"] = t_e
            rsl = rs4[:, par3 * 4:(par3 + 1) * 4]
            ril = ri4[:, par3 * 4:(par3 + 1) * 4]
            t_rs = P.add("vector",
                         lambda e, o=rsl,
                                s=EF[par3][:].rearrange("p (g cc) -> p g cc", g=4):
                         e.reduce_sum(o, s, axis=AX.X),
                         waits=mx(t_e, prev.get(("rs", par3))), inc="SV")
            t_ri = P.add("vector", lambda e, o=ril, s=rsl: e.reciprocal(o, s),
                         waits=mx(t_rs, prev.get(("ri", par3))), inc="SV")
            prev[("rs", par3)] = t_ri
            t_pms = []
            for j in range(4):
                t_pm = P.add("vector",
                             lambda e, o=PBM[par3][:, j * 128:(j + 1) * 128],
                                    s=EF[par3][:, j * 128:(j + 1) * 128],
                                    r=ril[:, j:j + 1]:
                             e.tensor_scalar(o, s, r, None, ALU.mult),
                             waits=mx(t_ri, prev.get(("pbm", par3))) if j == 0 else [], inc="SV")
                t_pms.append(t_pm)
            state["t_mu_last"] = t_pms[-1]
            prev[("ef", par3)] = t_pms[-1]
            prev[("ri", par3)] = t_pms[-1]
            pendA.append((b, t_pms))
        while pendA or pendB:
            if pendB:
                do_tailB(*pendB.pop(0))
            if pendA:
                do_tailA(*pendA.pop(0))
        st_["attn_pe_done"] = state["last_pe"]
        st_["attn_scr_free"] = mx(state["t_e_last"], state["t_mu_last"], state["last_pe"])
        return mx(*ev_toks)

    def ffn_phase(c, ready, attn_done):
        mult_toks = []
        prev_mu = [None, None]
        last_mm = None
        for hb in range(HCN):
            sl, tokd = stream_w(dwgu[hb])
            for tb in range(2):
                gbank = PBF[tb]
                ubank = PBF[2 + tb]
                toks_mm = {}
                for which, bank, coff in (("g", gbank, 0), ("u", ubank, 2048)):
                    wt = mx(tokd, bank_free[id(bank)], ready[tb])
                    t_mm = None
                    for kc in range(KCN):
                        t_mm = P.add("tensor",
                                     lambda e, o=bank[:],
                                            s=WST[sl][:, coff + kc * 128: coff + (kc + 1) * 128],
                                            r=xnT[:, kc * 1024 + tb * 512: kc * 1024 + tb * 512 + 512],
                                            stt=(kc == 0), sp=(kc == KCN - 1):
                                     e.matmul(o, s, r, start=stt, stop=sp),
                                     waits=wt if kc == 0 else [], inc="ST" if kc == KCN - 1 else None)
                    toks_mm[which] = t_mm
                    last_mm = t_mm
                wslot_free[sl] = toks_mm["u"]
                gb = gate_b[:, tb * 512:(tb + 1) * 512]
                ub = up_b[:, tb * 512:(tb + 1) * 512]
                t_si = P.add("scalar",
                             lambda e, o=gb, s=gbank[:], sc=sgT[:, hb:hb + 1]:
                             e.activation(o, s, AF.Silu, scale=sc),
                             waits=mx(toks_mm["g"], tok_sg, prev_mu[tb],
                                      st_["attn_scr_free"]), inc="SS")
                bank_free[id(gbank)] = t_si
                t_ue = P.add("scalar",
                             lambda e, o=ub, s=ubank[:], sc=suT[:, hb:hb + 1]:
                             e.activation(o, s, AF.Copy, scale=sc),
                             waits=mx(toks_mm["u"], tok_su, prev_mu[tb],
                                      st_["attn_scr_free"]), inc="SS")
                bank_free[id(ubank)] = t_ue
                dst = actT[:, hb * 1024 + tb * 512: hb * 1024 + tb * 512 + 512]
                t_mu = P.add("vector", lambda e, o=dst, s0=gb, s1=ub: e.tensor_mul(o, s0, s1),
                             waits=mx(t_si, t_ue, st_["attn_pe_done"], attn_done,
                                      st_["actT_read"] if c > 0 else None), inc="SV")
                prev_mu[tb] = t_mu
                mult_toks.append(t_mu)
        st_["xnT_free"] = last_mm
        return mx(*mult_toks)

    def out_phase(c, act_done, attn_done):
        hilo2 = colsum(act_done, hilo_ac,
                       lambda k, tb: actT[:, k * 1024 + tb * 512: k * 1024 + tb * 512 + 512],
                       HCN, p0,
                       war=mx(st_["actT_read"] if c > 0 else None))
        last_wo = None
        last_dn = None
        corr_toks = [None, None]
        for tb in range(2):
            pcb = [pb6, pb7][tb]
            pcf = [pf6, pf7][tb]
            t_cm = P.add("tensor",
                         lambda e, o=pcf[0:128, 0:512], s=cneg[:], r=hilo_ac[:, tb * 512:(tb + 1) * 512]:
                         e.matmul(o, s, r, start=True, stop=True),
                         waits=mx(hilo2, tok_cneg, bank_free[id(pcb)]), inc="ST")
            t_ce = P.add("scalar",
                         lambda e, o=corr_sb[tb][:], s=pcf[0:128, 0:512]: e.activation(o, s, AF.Copy),
                         waits=mx(t_cm, st_["corr_free"][tb]), inc="SS")
            bank_free[id(pcb)] = t_ce
            corr_toks[tb] = t_ce
        for tb in range(2):
            rows = c * CH + tb * 512
            pre_toks = []
            wpre = mx(act_done, st_["xnT_free"], st_["out_store"])
            for fs in range(4):
                dnvf = out_nat[:].rearrange("p (j cc) -> p j cc", j=4)[:, :, fs * 512:(fs + 1) * 512]
                srcxf = dx[rows:rows + 512, fs * 512:(fs + 1) * 512].rearrange(
                    "(j p) cc -> p j cc", p=128)
                pre_toks.append(P.add("sync" if fs % 2 == 0 else "scalar",
                                      lambda e, o=dnvf, s=srcxf: e.dma_start(o, s),
                                      waits=wpre, inc="dn", amt=16))
            add_toks = []
            prev_add = [None]
            pend = [None]

            def do_tail2(ot, tp, pbank, t_st, oc):
                t_tr = None
                for j in range(4):
                    t_tr = P.add("tensor",
                                 lambda e, o=tp[:, j * 128:(j + 1) * 128],
                                        s=ot[:, j * 128:(j + 1) * 128]:
                                 e.transpose(o, s, eye_b[:]),
                                 waits=mx(t_st, bank_free[id(pbank)]) if j == 0 else [],
                                 inc="ST" if j == 3 else None)
                st_["outT_free"][oc % 2] = t_tr
                dnv3 = out_nat[:].rearrange("p (j cc) -> p j cc", j=4)[:, :, oc * 128:(oc + 1) * 128]
                pv3 = tp[:, 0:512].rearrange("p (j cc) -> p j cc", j=4)
                t_ad = P.add("vector",
                             lambda e, o=dnv3, s0=pv3, s1=dnv3: e.tensor_tensor(o, s0, s1, ALU.add),
                             waits=mx(t_tr, pre_toks[oc // 4], prev_add[0]), inc="SV")
                bank_free[id(pbank)] = t_ad
                prev_add[0] = t_ad
                add_toks.append(t_ad)

            for oc in range(16):
                if oc % 2 == 0:
                    wo_sl, wo_tok = stream_w(dwo[oc // 2])
                d_sl0, d_tok0 = stream_d(dwd[oc][:, 0:2816],
                                         waits_extra=(st_["ln_xbuf_done"], st_["attn_scr_free"],
                                                      act_done))
                d_sl1, d_tok1 = stream_d(dwd[oc][:, 2816:5632],
                                         waits_extra=(st_["ln_xbuf_done"], st_["attn_scr_free"],
                                                      act_done))
                b = oc % 2
                wobank = PBF[b]
                dbank = PBF[2 + b]
                wt = mx(wo_tok, bank_free[id(wobank)], attn_done)
                t_mm = None
                for kc in range(KCN):
                    t_mm = P.add("tensor",
                                 lambda e, o=wobank[:],
                                        s=WST[wo_sl][:, b * 2048 + kc * 128: b * 2048 + (kc + 1) * 128],
                                        r=attnT[:, kc * 1024 + tb * 512: kc * 1024 + tb * 512 + 512],
                                        stt=(kc == 0), sp=(kc == KCN - 1):
                                 e.matmul(o, s, r, start=stt, stop=sp),
                                 waits=wt if kc == 0 else [], inc="ST" if kc == KCN - 1 else None)
                last_wo = t_mm
                if b == 1:
                    wslot_free[wo_sl] = t_mm
                wt = mx(d_tok0, bank_free[id(dbank)], act_done)
                wt22 = mx(d_tok1)
                t_dmm = None
                for hc in range(HCN):
                    piece_sl = d_sl0 if hc < 22 else d_sl1
                    cbase = (hc % 22) * 128
                    inc = "ST" if hc in (21, HCN - 1) else None
                    t_dmm = P.add("tensor",
                                  lambda e, o=dbank[:], s=DWST[piece_sl][:, cbase:cbase + 128],
                                         r=actT[:, hc * 1024 + tb * 512: hc * 1024 + tb * 512 + 512],
                                         stt=(hc == 0), sp=(hc == HCN - 1):
                                  e.matmul(o, s, r, start=stt, stop=sp),
                                  waits=(wt if hc == 0 else (wt22 if hc == 22 else [])), inc=inc)
                    if hc == 21:
                        dslot_free[d_sl0] = t_dmm
                        if tb == 1 and oc == 15:
                            st_["actT_read_early"] = t_dmm
                    if hc == HCN - 1:
                        dslot_free[d_sl1] = t_dmm
                last_dn = t_dmm
                ot = outT0 if oc % 2 == 0 else outT1
                t_cp = P.add("vector",
                             lambda e, o=ot[:], s0=corr_sb[tb][:], sc=sdT[:, oc:oc + 1],
                                    s1=wobank[:]:
                             e.scalar_tensor_tensor(o, s0, sc, s1, ALU.mult, ALU.add),
                             waits=mx(t_mm, corr_toks[tb], tok_sd,
                                      st_["outT_free"][oc % 2], st_["ln_xbuf_done"],
                                      st_["attn_scr_free"]), inc="SV")
                bank_free[id(wobank)] = t_cp
                t_st = P.add("vector",
                             lambda e, o=ot[:], s0=dbank[:], sc=sdT[:, oc:oc + 1], s1=ot[:]:
                             e.scalar_tensor_tensor(o, s0, sc, s1, ALU.mult, ALU.add),
                             waits=mx(t_dmm, t_cp, tok_sd), inc="SV")
                bank_free[id(dbank)] = t_st
                if pend[0] is not None:
                    do_tail2(*pend[0])
                pend[0] = (ot, pb4 if oc % 2 == 0 else pb5,
                           pb4 if oc % 2 == 0 else pb5, t_st, oc)
            do_tail2(*pend[0])
            st_["corr_free"][tb] = t_cp
            wA = mx(*add_toks[:8])
            wB = mx(*add_toks)
            for j in range(4):
                qa = ("sync", "scalar", "sync", "scalar")[j]
                P.add(qa,
                      lambda e, o=dy[rows + j * 128:rows + (j + 1) * 128, 0:1024],
                             s=out_nat[:, j * 2048:j * 2048 + 1024]:
                      e.dma_start(o, s),
                      waits=wA, inc="dout", amt=16)
            for j in range(4):
                qb = ("scalar", "sync", "scalar", "sync")[j]
                P.add(qb,
                      lambda e, o=dy[rows + j * 128:rows + (j + 1) * 128, 1024:2048],
                             s=out_nat[:, j * 2048 + 1024:(j + 1) * 2048]:
                      e.dma_start(o, s),
                      waits=wB, inc="dout", amt=16)
            st_["out_store"] = ("dout", P.cnt["dout"])
        st_["attnT_read"] = last_wo
        st_["actT_read"] = last_dn

    for c in range(NCH):
        ready = ln_phase(c)
        qkv_done = qkv_phase(c, ready)
        attn_done = attn_phase(c, qkv_done)
        act_done = ffn_phase(c, ready, attn_done)
        out_phase(c, act_done, attn_done)

    P.add("sync", lambda e: e.nop(), waits=[("dout", P.cnt["dout"])])

    import contextlib
    with contextlib.ExitStack() as stx:
        sems = {}
        for s in P.cnt:
            sems[s] = stx.enter_context(nc.semaphore(s))
        block = stx.enter_context(nc.Block())
        P.emit(block, sems)
    return nc


_NC_CACHE = {}


def _prep_weights(Wq, Wk, Wv, Wo, wq_gate, wq_up, wq_down):
    def qkv_tile(W):
        WT = np.ascontiguousarray(W.T)
        t = WT.reshape(16, 128, 16, 128).transpose(2, 1, 0, 3).reshape(16, 128, 2048)
        t = np.ascontiguousarray(
            t.reshape(8, 2, 128, 2048).transpose(0, 2, 1, 3).reshape(8, 128, 4096))
        return np.ascontiguousarray(
            t.view(np.uint8).reshape(8, 128, 4096, 4)[..., 2:4]).view(ml_dtypes.bfloat16).reshape(8, 128, 4096)

    def gu_tile(W):
        WT = np.ascontiguousarray(W.T)
        return WT.reshape(16, 128, 44, 128).transpose(2, 1, 0, 3).reshape(44, 128, 2048)

    wq = qkv_tile(Wq)
    wk = qkv_tile(Wk)
    wv = qkv_tile(Wv)
    wo = qkv_tile(Wo)
    wgu = np.ascontiguousarray(np.concatenate([gu_tile(wq_gate), gu_tile(wq_up)], axis=2))
    WdT = np.ascontiguousarray(wq_down.T)
    wd = np.ascontiguousarray(
        WdT.reshape(44, 128, 16, 128).transpose(2, 1, 0, 3).reshape(16, 128, 5632))
    wgu = np.ascontiguousarray(wgu.view(np.uint8).reshape(44, 128, 4096, 4)[..., 0])
    wd = np.ascontiguousarray(wd.view(np.uint8).reshape(16, 128, 5632, 4)[..., 0])
    return wq, wk, wv, wo, wgu, wd


def kernel(L0_attn_x, L0_attn_Wq, L0_attn_Wk, L0_attn_Wv, L0_attn_Wo,
           L0_ffn_wq_gate, L0_ffn_s_gate, L0_ffn_wq_up, L0_ffn_s_up,
           L0_ffn_wq_down, L0_ffn_s_down):
    x = np.ascontiguousarray(np.asarray(L0_attn_x, dtype=np.float32).reshape(16384, 2048))
    wq, wk, wv, wo, wgu, wd = _prep_weights(
        np.asarray(L0_attn_Wq, np.float32), np.asarray(L0_attn_Wk, np.float32),
        np.asarray(L0_attn_Wv, np.float32), np.asarray(L0_attn_Wo, np.float32),
        np.asarray(L0_ffn_wq_gate, np.int32), np.asarray(L0_ffn_wq_up, np.int32),
        np.asarray(L0_ffn_wq_down, np.int32))
    sg = np.ascontiguousarray(np.asarray(L0_ffn_s_gate, np.float32).reshape(44, 128))
    su = np.ascontiguousarray(np.asarray(L0_ffn_s_up, np.float32).reshape(44, 128))
    sd = np.ascontiguousarray(np.asarray(L0_ffn_s_down, np.float32).reshape(16, 128))
    eye = np.eye(128, dtype=np.float32)
    idx = np.arange(128)
    C = float(2 ** 43)
    mska = np.ones((9, 128), dtype=ml_dtypes.bfloat16)
    mskb = np.full((9, 512), -C, dtype=ml_dtypes.bfloat16)
    for r in range(8):
        mska[1 + r] = (idx % 8 == r).astype(ml_dtypes.bfloat16)
        mskb[1 + r] = np.tile(((idx % 8 == r) * C).astype(ml_dtypes.bfloat16), 4)
    mskb[0] = -C

    if "nc" not in _NC_CACHE:
        _NC_CACHE["nc"] = build_program()
    nc = _NC_CACHE["nc"]

    in_maps = []
    for ci in range(8):
        shard = np.ascontiguousarray(x[ci * 2048:(ci + 1) * 2048])
        in_maps.append(dict(x=shard, wq=wq, wk=wk, wv=wv, wo=wo, wgu=wgu, wd=wd,
                            sg=sg, su=su, sd=sd, eye=eye, mska=mska, mskb=mskb))
    res = run_bass_kernel_spmd(nc, in_maps, core_ids=list(range(8)))
    y = np.concatenate([np.asarray(res.results[i]["y"]) for i in range(8)], axis=0)
    return y.reshape(2, 8192, 2048).astype(np.float32)


# version 27
# speedup vs baseline: 1.1960x; 1.1960x over previous
import numpy as np
import ml_dtypes
import concourse.bass as bass
import concourse.mybir as mybir
from concourse.bass_utils import run_bass_kernel_spmd

F32 = mybir.dt.float32
BF16 = mybir.dt.bfloat16
I32 = mybir.dt.int32
U8 = mybir.dt.uint8
AF = mybir.ActivationFunctionType
ALU = mybir.AluOpType
AX = mybir.AxisListType

D = 2048
HID = 5632
NH = 16
T = 2048
CH = 1024
NCH = 2
KCN = 16
HCN = 44
NGRP = CH // 8
NBAT = NGRP // 4
PERM = [(h % 4) * 4 + h // 4 for h in range(NH)]
RSQ = 0.08838834764831845
EPS = 1e-5
SHIFT = -8.0

ENGS = ("sync", "gpsimd", "tensor", "vector", "scalar")


class Plan:
    def __init__(self):
        self.ops = {e: [] for e in ENGS}
        self.cnt = {}

    def sem(self, name):
        self.cnt.setdefault(name, 0)

    def add(self, eng, fn, waits=(), inc=None, amt=1):
        w = [(s, c) for (s, c) in waits if s is not None and c > 0]
        for s, c in w:
            assert self.cnt.get(s, 0) >= c, f"wait on future token {s}={c} (have {self.cnt.get(s, 0)})"
        tok = None
        if inc is not None:
            self.cnt[inc] = self.cnt.get(inc, 0) + amt
            tok = (inc, self.cnt[inc])
        self.ops[eng].append((w, fn, inc, amt))
        return tok

    def emit(self, block, sems):
        def mk(name):
            def body(eng):
                have = {}
                for w, fn, inc, amt in self.ops[name]:
                    for s, c in w:
                        if have.get(s, 0) < c:
                            eng.wait_ge(sems[s], c)
                            have[s] = c
                    ins = fn(eng)
                    if inc is not None:
                        ins.then_inc(sems[inc], amt)
            return body
        block.sync(mk("sync"))
        block.gpsimd(mk("gpsimd"))
        block.tensor(mk("tensor"))
        block.vector(mk("vector"))
        block.scalar(mk("scalar"))


def mx(*toks):
    best = {}
    for t in toks:
        if t is None:
            continue
        for it in (t if isinstance(t, list) else [t]):
            if it is None:
                continue
            s, c = it
            if best.get(s, 0) < c:
                best[s] = c
    return list(best.items())


def build_program():
    nc = bass.Bass("TRN2", target_bir_lowering=False)

    dx = nc.dram_tensor("x", [T, D], F32, kind="ExternalInput").ap()
    dwq = nc.dram_tensor("wq", [8, 128, 4096], BF16, kind="ExternalInput").ap()
    dwk = nc.dram_tensor("wk", [8, 128, 4096], BF16, kind="ExternalInput").ap()
    dwv = nc.dram_tensor("wv", [8, 128, 4096], BF16, kind="ExternalInput").ap()
    dwo = nc.dram_tensor("wo", [8, 128, 4096], BF16, kind="ExternalInput").ap()
    dwgu = nc.dram_tensor("wgu", [HCN, 128, 4096], U8, kind="ExternalInput").ap()
    dwd = nc.dram_tensor("wd", [16, 128, HID], U8, kind="ExternalInput").ap()
    dsg = nc.dram_tensor("sg", [HCN, 128], F32, kind="ExternalInput").ap()
    dsu = nc.dram_tensor("su", [HCN, 128], F32, kind="ExternalInput").ap()
    dsd = nc.dram_tensor("sd", [16, 128], F32, kind="ExternalInput").ap()
    deye = nc.dram_tensor("eye", [128, 128], F32, kind="ExternalInput").ap()
    dmska = nc.dram_tensor("mska", [9, 128], BF16, kind="ExternalInput").ap()
    dmskb = nc.dram_tensor("mskb", [9, 512], BF16, kind="ExternalInput").ap()
    dy = nc.dram_tensor("y", [T, D], F32, kind="ExternalOutput").ap()

    BASE = 17408

    def at(name, shape, dt, o):
        return nc.alloc_sbuf_tensor_at(name, shape, dt, offset=BASE + o)

    xnT = at("xnT", [128, 16384], BF16, 0)
    out_nat = at("out_nat", [128, 8192], F32, 0)
    Qb = at("Qb", [128, 16384], BF16, 32768)
    Kb = at("Kb", [128, 16384], BF16, 65536)
    Vb = at("Vb", [128, 16384], BF16, 98304)
    actT = at("actT", [128, 45056], BF16, 32768)
    xstg0 = at("xstg0", [128, 2048], F32, 32768)
    xstg1 = at("xstg1", [128, 2048], F32, 40960)
    xn_bs = [at(f"xn_b{k}", [128, 2048], BF16, 49152 + 4096 * k) for k in range(4)]
    attnT = at("attnT", [128, 16384], BF16, 131072)
    A0 = 163840
    xbuf0 = at("xbuf0", [128, 2048], F32, A0)
    xbuf1 = at("xbuf1", [128, 2048], F32, A0 + 8192)
    e_f0 = at("e_f0", [128, 512], F32, A0)
    e_f1 = at("e_f1", [128, 512], F32, A0 + 2048)
    e_f2 = at("e_f2", [128, 512], F32, A0 + 4096)
    p_b0 = at("p_b0", [128, 512], BF16, A0 + 6144)
    p_b1 = at("p_b1", [128, 512], BF16, A0 + 7168)
    p_b2 = at("p_b2", [128, 512], BF16, A0 + 8192)
    pT_s0 = at("pT_s0", [128, 512], BF16, A0 + 9216)
    pT_s1 = at("pT_s1", [128, 512], BF16, A0 + 10240)
    pT_s2 = at("pT_s2", [128, 512], BF16, A0 + 11264)
    vT_s0 = at("vT_s0", [128, 512], BF16, A0 + 12288)
    vT_s1 = at("vT_s1", [128, 512], BF16, A0 + 13312)
    vT_s2 = at("vT_s2", [128, 512], BF16, A0 + 14336)
    gate_b = at("gate_b", [128, 1024], BF16, A0)
    up_b = at("up_b", [128, 1024], BF16, A0 + 2048)
    dwst0 = at("dwst0", [128, 2816], BF16, A0)
    dwst1 = at("dwst1", [128, 2816], BF16, A0 + 5632)
    outT0 = at("outT0", [128, 512], BF16, A0 + 14336)
    outT1 = at("outT1", [128, 512], BF16, A0 + 15360)
    wst0 = at("wst0", [128, 4096], BF16, 180224)
    wst1 = at("wst1", [128, 4096], BF16, 188416)
    m = 196608

    def ata(name, shape, dt, nbytes):
        nonlocal m
        m = (m + 63) // 64 * 64
        t = at(name, shape, dt, m)
        m += nbytes
        return t
    ata("pad0", [1, 32], F32, 128)
    eye_f = ata("eye_f", [128, 128], F32, 512)
    eye_b = ata("eye_b", [128, 128], BF16, 256)
    mkA = ata("mkA", [9, 128], BF16, 256)
    mkB = ata("mkB", [9, 512], BF16, 1024)
    shift_t = ata("shift_t", [128, 1], F32, 16)
    sgT = ata("sgT", [128, 44], F32, 192)
    suT = ata("suT", [128, 44], F32, 192)
    sdT = ata("sdT", [128, 16], F32, 64)
    eps_t = ata("eps_t", [128, 1], F32, 16)
    ones_c = ata("ones_c", [128, 1], BF16, 16)
    cneg = ata("cneg", [2, 128], BF16, 256)
    bn6 = ata("bn6", [128, 96], F32, 384)
    mv2 = ata("mv2", [128, 8], F32, 32)
    stdt = ata("stdt", [128, 4], F32, 16)
    rstd = ata("rstd", [128, 4], F32, 16)
    nmr = ata("nmr", [128, 4], F32, 16)
    rs4 = ata("rs4", [128, 12], F32, 48)
    ri4 = ata("ri4", [128, 12], F32, 48)
    hilo_xn = ata("hilo_xn", [2, 1024], BF16, 2048)
    hilo_ac = hilo_xn
    lo_b = ata("lo_b", [1, 512], BF16, 1024)
    scl_stage = ata("scl_stage", [44, 384], F32, 1536)
    corr_sb = [ata("corr0", [128, 512], F32, 2048), ata("corr1", [128, 512], F32, 2048)]
    assert BASE + m <= 229376 - 512, f"sbuf over budget: {BASE + m}"

    p0 = nc.alloc_psum_tensor("p0", [128, 512], F32)
    p1 = nc.alloc_psum_tensor("p1", [128, 512], F32)
    p2 = nc.alloc_psum_tensor("p2", [128, 512], F32)
    p3 = nc.alloc_psum_tensor("p3", [128, 512], F32)
    pb4 = nc.alloc_psum_tensor("pb4", [128, 1024], BF16)
    pb5 = nc.alloc_psum_tensor("pb5", [128, 1024], BF16)
    pb6 = nc.alloc_psum_tensor("pb6", [128, 1024], BF16)
    pb7 = nc.alloc_psum_tensor("pb7", [128, 1024], BF16)
    pf4 = pb4[:].bitcast(F32)
    pf5 = pb5[:].bitcast(F32)
    pf6 = pb6[:].bitcast(F32)
    pf7 = pb7[:].bitcast(F32)
    PBF = [p0, p1, p2, p3]
    ALLP = [p0, p1, p2, p3, pb4, pb5, pb6, pb7]

    P = Plan()
    for s in ("SV", "SS", "ST", "dxa", "dxb", "dsc", "dhl", "dn", "dout",
              "wa", "wb", "da", "db", "ca", "cb", "cc", "cd", "dsl", "dmk",
              "dxe", "dxf", "cg", "ch"):
        P.sem(s)

    bank_free = {id(b): None for b in ALLP}
    WST = [wst0, wst1]
    wslot_free = [None, None]
    wslot_sem = ["wa", "wb"]
    wct = {"i": 0}
    DWST = [dwst0, dwst1]
    dslot_free = [None, None]
    dslot_sem = ["da", "db"]
    dct = {"i": 0}
    XBUF = [xbuf0, xbuf1, xstg0, xstg1]
    xslot_free = [None, None, None, None]
    xslot_sem = ["dxa", "dxb", "ca", "cb"]
    xslot_sem2 = ["dxe", "dxf", "cg", "ch"]

    st_ = {
        "attn_pe_done": None,
        "attn_scr_free": None,
        "attnT_read": None,
        "actT_read": None,
        "out_store": None,
        "xnT_free": None,
        "ln_xbuf_done": None,
        "outT_free": [None, None],
        "corr_free": [None, None],
        "actT_read_early": None,
    }

    P.add("gpsimd", lambda e: e.dma_start(eye_f[:], deye), inc="dsc", amt=16)
    tok_eye = ("dsc", P.cnt["dsc"])
    P.add("gpsimd", lambda e: e.dma_start(scl_stage[0:44, 0:128], dsg), inc="dsl", amt=16)
    P.add("gpsimd", lambda e: e.dma_start(scl_stage[0:44, 128:256], dsu), inc="dsl", amt=16)
    P.add("gpsimd", lambda e: e.dma_start(scl_stage[0:16, 256:384], dsd), inc="dsl", amt=16)
    tok_scl = ("dsl", P.cnt["dsl"])
    P.add("gpsimd", lambda e: e.dma_start(mkA[:], dmska), inc="dmk", amt=16)
    P.add("gpsimd", lambda e: e.dma_start(mkB[:], dmskb), inc="dmk", amt=16)
    tok_mk = ("dmk", P.cnt["dmk"])
    tok_eyeb = P.add("vector", lambda e: e.tensor_copy(eye_b[:], eye_f[:]),
                     waits=mx(tok_eye), inc="SV")
    tok_eps = P.add("vector", lambda e: e.memset(eps_t[:], EPS), inc="SV")
    tok_ones = P.add("vector", lambda e: e.memset(ones_c[:], 1.0), inc="SV")
    tok_cneg = P.add("vector", lambda e: e.memset(cneg[:], -128.0), inc="SV")
    tok_shift = P.add("vector", lambda e: e.memset(shift_t[:], SHIFT), inc="SV")
    tk = P.add("tensor", lambda e: e.transpose(pf4[0:128, 0:44], scl_stage[0:44, 0:128], eye_f[0:44, 0:44]),
               waits=mx(tok_eye, tok_scl), inc="ST")
    tok_sg = P.add("scalar", lambda e: e.activation(sgT[:], pf4[0:128, 0:44], AF.Copy),
                   waits=mx(tk), inc="SS")
    tk = P.add("tensor", lambda e: e.transpose(pf4[0:128, 44:88], scl_stage[0:44, 128:256], eye_f[0:44, 0:44]),
               waits=mx(tok_sg), inc="ST")
    tok_su = P.add("scalar", lambda e: e.activation(suT[:], pf4[0:128, 44:88], AF.Copy),
                   waits=mx(tk), inc="SS")
    tk = P.add("tensor", lambda e: e.transpose(pf4[0:128, 88:104], scl_stage[0:16, 256:384], eye_f[0:16, 0:16]),
               waits=mx(tok_su), inc="ST")
    tok_sd = P.add("scalar", lambda e: e.activation(sdT[:], pf4[0:128, 88:104], AF.Copy),
                   waits=mx(tk), inc="SS")
    bank_free[id(pb4)] = tok_sd

    def stream_w(src_ap, waits_extra=()):
        i = wct["i"] % 2
        wct["i"] += 1
        dst = WST[i][:, 0:src_ap.shape[-1]]
        tokd = P.add("gpsimd", lambda e, d=dst, s=src_ap: e.dma_start(d, s),
                     waits=mx(wslot_free[i], *waits_extra), inc=wslot_sem[i], amt=16)
        return i, tokd

    def stream_d(src_ap, waits_extra=()):
        i = dct["i"] % 2
        dct["i"] += 1
        dst = DWST[i][:, 0:src_ap.shape[-1]]
        tokd = P.add("gpsimd", lambda e, d=dst, s=src_ap: e.dma_start(d, s),
                     waits=mx(dslot_free[i], *waits_extra), inc=dslot_sem[i], amt=16)
        return i, tokd

    def ln_phase(c):
        xnT_w = []
        prev_norm = [None, None, None, None]
        prev_aggr_rd = [None, None, None, None]
        prev_tr_last = [None, None, None, None]
        toks_x = [None] * 8

        def emit_x(i):
            xs = i % 4
            rows = c * CH + i * 128
            if xs < 2:
                w = mx(xslot_free[xs],
                       (dslot_free[0] if c > 0 else None),
                       (dslot_free[1] if c > 0 else None),
                       (st_["outT_free"][0] if c > 0 else None),
                       (st_["outT_free"][1] if c > 0 else None))
            else:
                w = mx(xslot_free[xs],
                       (st_["attn_pe_done"] if c > 0 else None),
                       (st_["actT_read_early"] if c > 0 else None))
            if i < 2:
                for q0, q1 in ((0, 32), (64, 96)):
                    P.add("sync",
                          lambda e, d=XBUF[xs][q0:q0 + 32, :], s=dx[rows + q0:rows + q0 + 32, :]:
                          e.dma_start(d, s), waits=w, inc=xslot_sem[xs], amt=16)
                    P.add("scalar",
                          lambda e, d=XBUF[xs][q1:q1 + 32, :], s=dx[rows + q1:rows + q1 + 32, :]:
                          e.dma_start(d, s), waits=w, inc=xslot_sem2[xs], amt=16)
                ta = (xslot_sem[xs], P.cnt[xslot_sem[xs]])
                tb = (xslot_sem2[xs], P.cnt[xslot_sem2[xs]])
            else:
                ta = P.add("sync",
                           lambda e, d=XBUF[xs][0:64, :], s=dx[rows:rows + 64, :]:
                           e.dma_start(d, s), waits=w, inc=xslot_sem[xs], amt=16)
                tb = P.add("scalar",
                           lambda e, d=XBUF[xs][64:128, :], s=dx[rows + 64:rows + 128, :]:
                           e.dma_start(d, s), waits=w, inc=xslot_sem2[xs], amt=16)
            toks_x[i] = (ta, tb)

        tnorm_all = [None] * 8
        PBQ = [pb4, pb5, pb6, pb7]

        def stage1(i):
            sl = i % 4
            xs = i % 4
            tb4 = None
            for j in range(4):
                tb4 = P.add("vector",
                            lambda e, o=bn6[:, 24 * sl + 6 * j:24 * sl + 6 * j + 6],
                                   s=XBUF[xs][:, 512 * j:512 * (j + 1)]: e.bn_stats(o, s),
                            waits=mx(toks_x[i][0], toks_x[i][1], prev_aggr_rd[sl]), inc="SV")
            tagg = P.add("vector",
                         lambda e, o=mv2[:, 2 * sl:2 * sl + 2], s=bn6[:, 24 * sl:24 * sl + 24]:
                         e.bn_aggr(o, s),
                         waits=mx(tb4, prev_norm[sl]), inc="SV")
            tsq = P.add("scalar",
                        lambda e, o=stdt[:, sl:sl + 1], s=mv2[:, 2 * sl + 1:2 * sl + 2]:
                        e.activation(o, s, AF.Sqrt, bias=eps_t[:]),
                        waits=mx(tagg, tok_eps, prev_norm[sl]), inc="SS")
            trc = P.add("vector",
                        lambda e, o=rstd[:, sl:sl + 1], s=stdt[:, sl:sl + 1]: e.reciprocal(o, s),
                        waits=mx(tsq, prev_norm[sl]), inc="SV")
            tnm = P.add("vector",
                        lambda e, o=nmr[:, sl:sl + 1], s=mv2[:, 2 * sl:2 * sl + 1],
                               r=rstd[:, sl:sl + 1]:
                        e.tensor_scalar(o, s, r, -1.0, ALU.mult, ALU.mult),
                        waits=mx(trc, prev_norm[sl]), inc="SV")
            xnb = xn_bs[sl]
            wnorm = mx(tagg, trc, tnm, prev_tr_last[sl],
                       st_["attn_pe_done"] if c > 0 else None,
                       st_["actT_read_early"] if c > 0 else None)
            tnorm_h = []
            for hh in range(2):
                tnorm = P.add("scalar",
                              lambda e, o=xnb[:, hh * 1024:(hh + 1) * 1024],
                                     s=XBUF[xs][:, hh * 1024:(hh + 1) * 1024],
                                     b=nmr[:, sl:sl + 1], r=rstd[:, sl:sl + 1]:
                              e.activation(o, s, AF.Identity, bias=b, scale=r),
                              waits=wnorm, inc="SS")
                tnorm_h.append(tnorm)
            tnorm_all[i] = tnorm_h
            prev_norm[sl] = tnorm_h[1]
            prev_aggr_rd[sl] = tnorm_h[1]
            xslot_free[xs] = tnorm_h[1]
            if i + 4 < 8:
                emit_x(i + 4)

        def stage2(i):
            sl = i % 4
            xnb = xn_bs[sl]
            tnorm_h = tnorm_all[i]
            last_tr = None
            for h in range(2):
                pb = PBQ[(i * 2 + h) % 4]
                t_tr = None
                for j in range(8):
                    kc = h * 8 + j
                    t_tr = P.add("tensor",
                                 lambda e, o=pb[:, j * 128:(j + 1) * 128],
                                        s=xnb[:, kc * 128:(kc + 1) * 128]:
                                 e.transpose(o, s, eye_b[:]),
                                 waits=mx(tnorm_h[h], tok_eyeb, bank_free[id(pb)]) if j == 0 else [],
                                 inc="ST" if j == 7 else None)
                dst = xnT[:].rearrange("p (k t) -> p k t", t=1024)[:, h * 8:(h + 1) * 8,
                                                                  i * 128:(i + 1) * 128]
                src = pb[:, 0:1024].rearrange("p (k t) -> p k t", t=128)
                wv = mx(t_tr, st_["out_store"] if c > 0 else None)
                if h == 0:
                    ev = P.add("scalar", lambda e, o=dst, s=src: e.activation(o, s, AF.Copy),
                               waits=wv, inc="SS")
                else:
                    ev = P.add("vector", lambda e, o=dst, s=src: e.tensor_copy(o, s),
                               waits=wv, inc="SV")
                bank_free[id(pb)] = ev
                xnT_w.append((i, ev))
                last_tr = t_tr
            prev_tr_last[sl] = last_tr

        for i in range(4):
            emit_x(i)
        for i in range(8):
            stage1(i)
            if i >= 1:
                stage2(i - 1)
        stage2(7)
        st_["ln_xbuf_done"] = mx(prev_norm[0], prev_norm[1])
        r0 = mx(*[t for (i, t) in xnT_w if i < 4])
        r1 = mx(*[t for (i, t) in xnT_w])
        return (r0, r1)

    def colsum(ready, dst_hilo, mov_fn, nacc, pbanks, war):
        toks = []
        for tb in range(2):
            pbank = pbanks[tb]
            wt = mx(ready, bank_free[id(pbank)], tok_ones, war)
            t_mm = None
            for k in range(nacc):
                t_mm = P.add("tensor",
                             lambda e, o=pbank[0:1, 0:512], s=ones_c[:], r=mov_fn(k, tb),
                                    stt=(k == 0), sp=(k == nacc - 1):
                             e.matmul(o, s, r, start=stt, stop=sp),
                             waits=wt if k == 0 else [], inc="ST" if k == nacc - 1 else None)
            th = P.add("vector",
                       lambda e, o=dst_hilo[0:1, tb * 512:(tb + 1) * 512], s=pbank[0:1, 0:512]:
                       e.tensor_copy(o, s), waits=mx(t_mm, war, [("dhl", P.cnt["dhl"])]), inc="SV")
            tl = P.add("vector",
                       lambda e, o=lo_b[0:1, :], s0=pbank[0:1, 0:512],
                              s1=dst_hilo[0:1, tb * 512:(tb + 1) * 512]:
                       e.tensor_sub(o, s0, s1), waits=mx(th), inc="SV")
            bank_free[id(pbank)] = tl
            td = P.add("sync",
                       lambda e, o=dst_hilo[1:2, tb * 512:(tb + 1) * 512], s=lo_b[0:1, :]:
                       e.dma_start(o, s), waits=mx(tl), inc="dhl", amt=16)
            toks.append(td)
        return toks

    def qkv_phase(c, ready):
        evac_toks = []
        for dw, dstb, qperm in ((dwq, Qb, True), (dwk, Kb, False), (dwv, Vb, False)):
            for pr in range(8):
                sl, tokd = stream_w(dw[pr])
                for b in range(2):
                    ob = pr * 2 + b
                    for tb in range(2):
                        bank = PBF[b * 2 + tb]
                        wt = mx(tokd, bank_free[id(bank)], ready[tb])
                        t_mm = None
                        for kc in range(KCN):
                            t_mm = P.add("tensor",
                                         lambda e, o=bank[:],
                                                s=WST[sl][:, b * 2048 + kc * 128: b * 2048 + (kc + 1) * 128],
                                                r=xnT[:, kc * 1024 + tb * 512: kc * 1024 + tb * 512 + 512],
                                                stt=(kc == 0), sp=(kc == KCN - 1):
                                         e.matmul(o, s, r, start=stt, stop=sp),
                                         waits=wt if kc == 0 else [], inc="ST" if kc == KCN - 1 else None)
                        wslot_free[sl] = t_mm
                        slot_h = PERM[ob] if qperm else ob
                        dview = dstb[:].rearrange("p (g s i) -> p g s i", s=NH, i=8)
                        dst = dview[:, tb * 64:(tb + 1) * 64, slot_h, :]
                        src = bank[:].rearrange("p (g i) -> p g i", i=8)
                        wv = mx(t_mm,
                                st_["attn_pe_done"] if c > 0 else None,
                                st_["actT_read"] if c > 0 else None)
                        if (ob + tb) % 2 == 0:
                            ev = P.add("scalar", lambda e, o=dst, s=src: e.activation(o, s, AF.Copy),
                                       waits=wv, inc="SS")
                        else:
                            ev = P.add("vector", lambda e, o=dst, s=src: e.tensor_copy(o, s),
                                       waits=wv, inc="SV")
                        bank_free[id(bank)] = ev
                        evac_toks.append(ev)
        return mx(*evac_toks)

    def attn_phase(c, qkv_done):
        EF = [e_f0, e_f1, e_f2]
        PBM = [p_b0, p_b1, p_b2]
        PTS = [pT_s0, pT_s1, pT_s2]
        VTS = [vT_s0, vT_s1, vT_s2]
        PSC = [p0, p1]
        PAV = [p2, p3]
        prev = {}
        ev_toks = []
        state = {"last_pe": None, "t_e_last": None, "t_mu_last": None}
        pendA = []
        pendB = []

        def do_tailA(bb, t_pm_toks):
            par = bb % 2
            par3 = bb % 3
            pPT = pb4 if par == 0 else pb5
            t_tr = None
            for j in range(4):
                t_tr = P.add("tensor",
                             lambda e, o=pPT[:, j * 128:(j + 1) * 128],
                                    s=PBM[par3][:, j * 128:(j + 1) * 128]:
                             e.transpose(o, s, eye_b[:]),
                             waits=mx(t_pm_toks[j],
                                      bank_free[id(pPT)] if j == 0 else None,
                                      tok_eyeb if j == 0 else None),
                             inc="ST" if j == 3 else None)
            prev[("pbm", par3)] = t_tr
            t_pte = P.add("scalar", lambda e, o=PTS[par3][:], s=pPT[:, 0:512]: e.activation(o, s, AF.Copy),
                          waits=mx(t_tr, prev.get(("pts", par3))), inc="SS")
            bank_free[id(pPT)] = t_pte
            pendB.append((bb, t_pte))

        def do_tailB(bb, t_pte):
            par3 = bb % 3
            pAV = PAV[bb % 2]
            t_av = None
            for j in range(4):
                t_av = P.add("tensor",
                             lambda e, o=pAV[:, j * 128:(j + 1) * 128],
                                    s=VTS[par3][:, j * 128:(j + 1) * 128],
                                    r=PTS[par3][:, j * 128:(j + 1) * 128]:
                             e.matmul(o, s, r, start=True, stop=True),
                             waits=mx(prev.get(("vte", par3)), t_pte, bank_free[id(pAV)]) if j == 0 else [],
                             inc="ST" if j == 3 else None)
            prev[("vts", par3)] = t_av
            prev[("pts", par3)] = t_av
            state["last_pe"] = t_av
            dst = attnT[:].rearrange("p (h gg i) -> p h gg i", gg=NGRP, i=8)[:, :, bb * 4:(bb + 1) * 4, :]
            src = pAV[:].rearrange("p (g h i) -> p h g i", g=4, i=8)
            t_ev = P.add("scalar", lambda e, o=dst, s=src: e.activation(o, s, AF.Copy),
                         waits=mx(t_av, prev.get(("avev",)),
                                  st_["attnT_read"] if c > 0 else None), inc="SS")
            prev[("avev",)] = t_ev
            bank_free[id(pAV)] = t_ev
            ev_toks.append(t_ev)

        for b in range(NBAT):
            par = b % 2
            par3 = b % 3
            pS = PSC[b % 2]
            pVT = pb6 if par == 0 else pb7
            for j in range(4):
                g = b * 4 + j
                P.add("tensor",
                      lambda e, o=pS[:, j * 128:(j + 1) * 128],
                             s=Qb[:, g * 128:(g + 1) * 128], r=Kb[:, g * 128:(g + 1) * 128],
                             stt=(j == 0):
                      e.matmul(o, s, r, start=stt, stop=False),
                      waits=mx(qkv_done, bank_free[id(pS)]) if j == 0 else [],
                      inc=None)
            t_sc = P.add("tensor",
                         lambda e, o=pS[:], s=mkA[:], r=mkB[:]:
                         e.matmul(o, s, r, start=False, stop=True),
                         waits=mx(tok_mk), inc="ST")
            t_vt = None
            for j in range(4):
                g = b * 4 + j
                t_vt = P.add("tensor",
                             lambda e, o=pVT[:, j * 128:(j + 1) * 128], s=Vb[:, g * 128:(g + 1) * 128]:
                             e.transpose(o, s, eye_b[:]),
                             waits=mx(qkv_done, bank_free[id(pVT)], tok_eyeb) if j == 0 else [],
                             inc="ST" if j == 3 else None)
            if len(pendA) == 2:
                do_tailA(*pendA.pop(0))
            if len(pendB) == 2:
                do_tailB(*pendB.pop(0))
            if b % 2 == 0:
                t_vte = P.add("vector", lambda e, o=VTS[par3][:], s=pVT[:, 0:512]: e.tensor_copy(o, s),
                              waits=mx(t_vt, prev.get(("vts", par3))), inc="SV")
            else:
                t_vte = P.add("scalar", lambda e, o=VTS[par3][:], s=pVT[:, 0:512]: e.activation(o, s, AF.Copy),
                              waits=mx(t_vt, prev.get(("vts", par3))), inc="SS")
            prev[("vte", par3)] = t_vte
            bank_free[id(pVT)] = t_vte
            t_e = P.add("scalar",
                        lambda e, o=EF[par3][:], s=pS[:]:
                        e.activation(o, s, AF.Exp, bias=shift_t[:], scale=RSQ),
                        waits=mx(t_sc, tok_shift, prev.get(("ef", par3)),
                                 st_["ln_xbuf_done"]), inc="SS")
            bank_free[id(pS)] = t_e
            state["t_e_last"] = t_e
            rsl = rs4[:, par3 * 4:(par3 + 1) * 4]
            ril = ri4[:, par3 * 4:(par3 + 1) * 4]
            t_rs = P.add("vector",
                         lambda e, o=rsl,
                                s=EF[par3][:].rearrange("p (g cc) -> p g cc", g=4):
                         e.reduce_sum(o, s, axis=AX.X),
                         waits=mx(t_e, prev.get(("rs", par3))), inc="SV")
            t_ri = P.add("vector", lambda e, o=ril, s=rsl: e.reciprocal(o, s),
                         waits=mx(t_rs, prev.get(("ri", par3))), inc="SV")
            prev[("rs", par3)] = t_ri
            t_pms = []
            for j in range(4):
                t_pm = P.add("vector",
                             lambda e, o=PBM[par3][:, j * 128:(j + 1) * 128],
                                    s=EF[par3][:, j * 128:(j + 1) * 128],
                                    r=ril[:, j:j + 1]:
                             e.tensor_scalar(o, s, r, None, ALU.mult),
                             waits=mx(t_ri, prev.get(("pbm", par3))) if j == 0 else [], inc="SV")
                t_pms.append(t_pm)
            state["t_mu_last"] = t_pms[-1]
            prev[("ef", par3)] = t_pms[-1]
            prev[("ri", par3)] = t_pms[-1]
            pendA.append((b, t_pms))
        while pendA or pendB:
            if pendB:
                do_tailB(*pendB.pop(0))
            if pendA:
                do_tailA(*pendA.pop(0))
        st_["attn_pe_done"] = state["last_pe"]
        st_["attn_scr_free"] = mx(state["t_e_last"], state["t_mu_last"], state["last_pe"])
        return mx(*ev_toks)

    def ffn_phase(c, ready, attn_done):
        mult_toks = []
        prev_mu = [None, None]
        last_mm = None
        for hb in range(HCN):
            sl, tokd = stream_w(dwgu[hb])
            for tb in range(2):
                gbank = PBF[tb]
                ubank = PBF[2 + tb]
                toks_mm = {}
                for which, bank, coff in (("g", gbank, 0), ("u", ubank, 2048)):
                    wt = mx(tokd, bank_free[id(bank)], ready[tb])
                    t_mm = None
                    for kc in range(KCN):
                        t_mm = P.add("tensor",
                                     lambda e, o=bank[:],
                                            s=WST[sl][:, coff + kc * 128: coff + (kc + 1) * 128],
                                            r=xnT[:, kc * 1024 + tb * 512: kc * 1024 + tb * 512 + 512],
                                            stt=(kc == 0), sp=(kc == KCN - 1):
                                     e.matmul(o, s, r, start=stt, stop=sp),
                                     waits=wt if kc == 0 else [], inc="ST" if kc == KCN - 1 else None)
                    toks_mm[which] = t_mm
                    last_mm = t_mm
                wslot_free[sl] = toks_mm["u"]
                gb = gate_b[:, tb * 512:(tb + 1) * 512]
                ub = up_b[:, tb * 512:(tb + 1) * 512]
                t_si = P.add("scalar",
                             lambda e, o=gb, s=gbank[:], sc=sgT[:, hb:hb + 1]:
                             e.activation(o, s, AF.Silu, scale=sc),
                             waits=mx(toks_mm["g"], tok_sg, prev_mu[tb],
                                      st_["attn_scr_free"]), inc="SS")
                bank_free[id(gbank)] = t_si
                t_ue = P.add("scalar",
                             lambda e, o=ub, s=ubank[:], sc=suT[:, hb:hb + 1]:
                             e.activation(o, s, AF.Copy, scale=sc),
                             waits=mx(toks_mm["u"], tok_su, prev_mu[tb],
                                      st_["attn_scr_free"]), inc="SS")
                bank_free[id(ubank)] = t_ue
                dst = actT[:, hb * 1024 + tb * 512: hb * 1024 + tb * 512 + 512]
                t_mu = P.add("vector", lambda e, o=dst, s0=gb, s1=ub: e.tensor_mul(o, s0, s1),
                             waits=mx(t_si, t_ue, st_["attn_pe_done"], attn_done,
                                      st_["actT_read"] if c > 0 else None), inc="SV")
                prev_mu[tb] = t_mu
                mult_toks.append(t_mu)
        st_["xnT_free"] = last_mm
        return mx(*mult_toks)

    def out_phase(c, act_done, attn_done):
        hilo2 = colsum(act_done, hilo_ac,
                       lambda k, tb: actT[:, k * 1024 + tb * 512: k * 1024 + tb * 512 + 512],
                       HCN, (p0, p1),
                       war=mx(st_["actT_read"] if c > 0 else None))
        last_wo = None
        last_dn = None
        corr_toks = [None, None]
        for tb in range(2):
            pcb = [pb6, pb7][tb]
            pcf = [pf6, pf7][tb]
            t_cm = P.add("tensor",
                         lambda e, o=pcf[0:128, 0:512], s=cneg[:], r=hilo_ac[:, tb * 512:(tb + 1) * 512]:
                         e.matmul(o, s, r, start=True, stop=True),
                         waits=mx(hilo2, tok_cneg, bank_free[id(pcb)]), inc="ST")
            t_ce = P.add("scalar",
                         lambda e, o=corr_sb[tb][:], s=pcf[0:128, 0:512]: e.activation(o, s, AF.Copy),
                         waits=mx(t_cm, st_["corr_free"][tb]), inc="SS")
            bank_free[id(pcb)] = t_ce
            corr_toks[tb] = t_ce
        for tb in range(2):
            rows = c * CH + tb * 512
            dnv = out_nat[:].rearrange("p (j cc) -> p j cc", j=4)
            srcx = dx[rows:rows + 512, :].rearrange("(j p) cc -> p j cc", p=128)
            t_pre = P.add("sync", lambda e, o=dnv, s=srcx: e.dma_start(o, s),
                          waits=mx(act_done, st_["xnT_free"], st_["out_store"]),
                          inc="dn", amt=16)
            add_toks = []
            prev_add = [None]
            pend = [None]

            def do_tail2(ot, tp, pbank, t_st, oc):
                t_tr = None
                for j in range(4):
                    t_tr = P.add("tensor",
                                 lambda e, o=tp[:, j * 128:(j + 1) * 128],
                                        s=ot[:, j * 128:(j + 1) * 128]:
                                 e.transpose(o, s, eye_b[:]),
                                 waits=mx(t_st, bank_free[id(pbank)]) if j == 0 else [],
                                 inc="ST" if j == 3 else None)
                st_["outT_free"][oc % 2] = t_tr
                dnv3 = out_nat[:].rearrange("p (j cc) -> p j cc", j=4)[:, :, oc * 128:(oc + 1) * 128]
                pv3 = tp[:, 0:512].rearrange("p (j cc) -> p j cc", j=4)
                t_ad = P.add("vector",
                             lambda e, o=dnv3, s0=pv3, s1=dnv3: e.tensor_tensor(o, s0, s1, ALU.add),
                             waits=mx(t_tr, t_pre, prev_add[0]), inc="SV")
                bank_free[id(pbank)] = t_ad
                prev_add[0] = t_ad
                add_toks.append(t_ad)

            for oc in range(16):
                if oc % 2 == 0:
                    wo_sl, wo_tok = stream_w(dwo[oc // 2])
                d_sl0, d_tok0 = stream_d(dwd[oc][:, 0:2816],
                                         waits_extra=(st_["ln_xbuf_done"], st_["attn_scr_free"],
                                                      act_done))
                d_sl1, d_tok1 = stream_d(dwd[oc][:, 2816:5632],
                                         waits_extra=(st_["ln_xbuf_done"], st_["attn_scr_free"],
                                                      act_done))
                b = oc % 2
                wobank = PBF[b]
                dbank = PBF[2 + b]
                wt = mx(wo_tok, bank_free[id(wobank)], attn_done)
                t_mm = None
                for kc in range(KCN):
                    t_mm = P.add("tensor",
                                 lambda e, o=wobank[:],
                                        s=WST[wo_sl][:, b * 2048 + kc * 128: b * 2048 + (kc + 1) * 128],
                                        r=attnT[:, kc * 1024 + tb * 512: kc * 1024 + tb * 512 + 512],
                                        stt=(kc == 0), sp=(kc == KCN - 1):
                                 e.matmul(o, s, r, start=stt, stop=sp),
                                 waits=wt if kc == 0 else [], inc="ST" if kc == KCN - 1 else None)
                last_wo = t_mm
                if b == 1:
                    wslot_free[wo_sl] = t_mm
                wt = mx(d_tok0, bank_free[id(dbank)], act_done)
                wt22 = mx(d_tok1)
                t_dmm = None
                for hc in range(HCN):
                    piece_sl = d_sl0 if hc < 22 else d_sl1
                    cbase = (hc % 22) * 128
                    inc = "ST" if hc in (21, HCN - 1) else None
                    t_dmm = P.add("tensor",
                                  lambda e, o=dbank[:], s=DWST[piece_sl][:, cbase:cbase + 128],
                                         r=actT[:, hc * 1024 + tb * 512: hc * 1024 + tb * 512 + 512],
                                         stt=(hc == 0), sp=(hc == HCN - 1):
                                  e.matmul(o, s, r, start=stt, stop=sp),
                                  waits=(wt if hc == 0 else (wt22 if hc == 22 else [])), inc=inc)
                    if hc == 21:
                        dslot_free[d_sl0] = t_dmm
                        if tb == 1 and oc == 15:
                            st_["actT_read_early"] = t_dmm
                    if hc == HCN - 1:
                        dslot_free[d_sl1] = t_dmm
                last_dn = t_dmm
                ot = outT0 if oc % 2 == 0 else outT1
                t_cp = P.add("vector",
                             lambda e, o=ot[:], s0=corr_sb[tb][:], sc=sdT[:, oc:oc + 1],
                                    s1=wobank[:]:
                             e.scalar_tensor_tensor(o, s0, sc, s1, ALU.mult, ALU.add),
                             waits=mx(t_mm, corr_toks[tb], tok_sd,
                                      st_["outT_free"][oc % 2], st_["ln_xbuf_done"],
                                      st_["attn_scr_free"]), inc="SV")
                bank_free[id(wobank)] = t_cp
                t_st = P.add("vector",
                             lambda e, o=ot[:], s0=dbank[:], sc=sdT[:, oc:oc + 1], s1=ot[:]:
                             e.scalar_tensor_tensor(o, s0, sc, s1, ALU.mult, ALU.add),
                             waits=mx(t_dmm, t_cp, tok_sd), inc="SV")
                bank_free[id(dbank)] = t_st
                if pend[0] is not None:
                    do_tail2(*pend[0])
                pend[0] = (ot, pb4 if oc % 2 == 0 else pb5,
                           pb4 if oc % 2 == 0 else pb5, t_st, oc)
            do_tail2(*pend[0])
            st_["corr_free"][tb] = t_cp
            wA = mx(*add_toks[:8])
            wB = mx(*add_toks)
            last_store = (c == NCH - 1 and tb == 1)
            QA = ("sync", "scalar", "gpsimd", "sync") if last_store else ("sync", "scalar", "sync", "scalar")
            QB = ("scalar", "gpsimd", "sync", "scalar") if last_store else ("scalar", "sync", "scalar", "sync")
            for j in range(4):
                P.add(QA[j],
                      lambda e, o=dy[rows + j * 128:rows + (j + 1) * 128, 0:1024],
                             s=out_nat[:, j * 2048:j * 2048 + 1024]:
                      e.dma_start(o, s),
                      waits=wA, inc="dout", amt=16)
            for j in range(4):
                P.add(QB[j],
                      lambda e, o=dy[rows + j * 128:rows + (j + 1) * 128, 1024:2048],
                             s=out_nat[:, j * 2048 + 1024:(j + 1) * 2048]:
                      e.dma_start(o, s),
                      waits=wB, inc="dout", amt=16)
            st_["out_store"] = ("dout", P.cnt["dout"])
        st_["attnT_read"] = last_wo
        st_["actT_read"] = last_dn

    for c in range(NCH):
        ready = ln_phase(c)
        qkv_done = qkv_phase(c, ready)
        attn_done = attn_phase(c, qkv_done)
        act_done = ffn_phase(c, ready, attn_done)
        out_phase(c, act_done, attn_done)

    P.add("sync", lambda e: e.nop(), waits=[("dout", P.cnt["dout"])])

    import contextlib
    with contextlib.ExitStack() as stx:
        sems = {}
        for s in P.cnt:
            sems[s] = stx.enter_context(nc.semaphore(s))
        block = stx.enter_context(nc.Block())
        P.emit(block, sems)
    return nc


_NC_CACHE = {}


def _prep_weights(Wq, Wk, Wv, Wo, wq_gate, wq_up, wq_down):
    def qkv_tile(W):
        WT = np.ascontiguousarray(W.T)
        t = WT.reshape(16, 128, 16, 128).transpose(2, 1, 0, 3).reshape(16, 128, 2048)
        t = np.ascontiguousarray(
            t.reshape(8, 2, 128, 2048).transpose(0, 2, 1, 3).reshape(8, 128, 4096))
        return np.ascontiguousarray(
            t.view(np.uint8).reshape(8, 128, 4096, 4)[..., 2:4]).view(ml_dtypes.bfloat16).reshape(8, 128, 4096)

    def gu_tile(W):
        WT = np.ascontiguousarray(W.T)
        return WT.reshape(16, 128, 44, 128).transpose(2, 1, 0, 3).reshape(44, 128, 2048)

    wq = qkv_tile(Wq)
    wk = qkv_tile(Wk)
    wv = qkv_tile(Wv)
    wo = qkv_tile(Wo)
    wgu = np.ascontiguousarray(np.concatenate([gu_tile(wq_gate), gu_tile(wq_up)], axis=2))
    WdT = np.ascontiguousarray(wq_down.T)
    wd = np.ascontiguousarray(
        WdT.reshape(44, 128, 16, 128).transpose(2, 1, 0, 3).reshape(16, 128, 5632))
    wgu = np.ascontiguousarray(wgu.view(np.uint8).reshape(44, 128, 4096, 4)[..., 0])
    wd = np.ascontiguousarray(wd.view(np.uint8).reshape(16, 128, 5632, 4)[..., 0])
    return wq, wk, wv, wo, wgu, wd


def kernel(L0_attn_x, L0_attn_Wq, L0_attn_Wk, L0_attn_Wv, L0_attn_Wo,
           L0_ffn_wq_gate, L0_ffn_s_gate, L0_ffn_wq_up, L0_ffn_s_up,
           L0_ffn_wq_down, L0_ffn_s_down):
    x = np.ascontiguousarray(np.asarray(L0_attn_x, dtype=np.float32).reshape(16384, 2048))
    wq, wk, wv, wo, wgu, wd = _prep_weights(
        np.asarray(L0_attn_Wq, np.float32), np.asarray(L0_attn_Wk, np.float32),
        np.asarray(L0_attn_Wv, np.float32), np.asarray(L0_attn_Wo, np.float32),
        np.asarray(L0_ffn_wq_gate, np.int32), np.asarray(L0_ffn_wq_up, np.int32),
        np.asarray(L0_ffn_wq_down, np.int32))
    sg = np.ascontiguousarray(np.asarray(L0_ffn_s_gate, np.float32).reshape(44, 128))
    su = np.ascontiguousarray(np.asarray(L0_ffn_s_up, np.float32).reshape(44, 128))
    sd = np.ascontiguousarray(np.asarray(L0_ffn_s_down, np.float32).reshape(16, 128))
    eye = np.eye(128, dtype=np.float32)
    idx = np.arange(128)
    C = float(2 ** 43)
    mska = np.ones((9, 128), dtype=ml_dtypes.bfloat16)
    mskb = np.full((9, 512), -C, dtype=ml_dtypes.bfloat16)
    for r in range(8):
        mska[1 + r] = (idx % 8 == r).astype(ml_dtypes.bfloat16)
        mskb[1 + r] = np.tile(((idx % 8 == r) * C).astype(ml_dtypes.bfloat16), 4)
    mskb[0] = -C

    if "nc" not in _NC_CACHE:
        _NC_CACHE["nc"] = build_program()
    nc = _NC_CACHE["nc"]

    in_maps = []
    for ci in range(8):
        shard = np.ascontiguousarray(x[ci * 2048:(ci + 1) * 2048])
        in_maps.append(dict(x=shard, wq=wq, wk=wk, wv=wv, wo=wo, wgu=wgu, wd=wd,
                            sg=sg, su=su, sd=sd, eye=eye, mska=mska, mskb=mskb))
    res = run_bass_kernel_spmd(nc, in_maps, core_ids=list(range(8)))
    y = np.concatenate([np.asarray(res.results[i]["y"]) for i in range(8)], axis=0)
    return y.reshape(2, 8192, 2048).astype(np.float32)


# version 28 (final)
# speedup vs baseline: 1.1967x; 1.0006x over previous
import numpy as np
import ml_dtypes
import concourse.bass as bass
import concourse.mybir as mybir
from concourse.bass_utils import run_bass_kernel_spmd

F32 = mybir.dt.float32
BF16 = mybir.dt.bfloat16
I32 = mybir.dt.int32
U8 = mybir.dt.uint8
AF = mybir.ActivationFunctionType
ALU = mybir.AluOpType
AX = mybir.AxisListType

D = 2048
HID = 5632
NH = 16
T = 2048
CH = 1024
NCH = 2
KCN = 16
HCN = 44
NGRP = CH // 8
NBAT = NGRP // 4
PERM = [(h % 4) * 4 + h // 4 for h in range(NH)]
RSQ = 0.08838834764831845
EPS = 1e-5
SHIFT = -8.0

ENGS = ("sync", "gpsimd", "tensor", "vector", "scalar")


class Plan:
    def __init__(self):
        self.ops = {e: [] for e in ENGS}
        self.cnt = {}

    def sem(self, name):
        self.cnt.setdefault(name, 0)

    def add(self, eng, fn, waits=(), inc=None, amt=1):
        w = [(s, c) for (s, c) in waits if s is not None and c > 0]
        for s, c in w:
            assert self.cnt.get(s, 0) >= c, f"wait on future token {s}={c} (have {self.cnt.get(s, 0)})"
        tok = None
        if inc is not None:
            self.cnt[inc] = self.cnt.get(inc, 0) + amt
            tok = (inc, self.cnt[inc])
        self.ops[eng].append((w, fn, inc, amt))
        return tok

    def emit(self, block, sems):
        def mk(name):
            def body(eng):
                have = {}
                for w, fn, inc, amt in self.ops[name]:
                    for s, c in w:
                        if have.get(s, 0) < c:
                            eng.wait_ge(sems[s], c)
                            have[s] = c
                    ins = fn(eng)
                    if inc is not None:
                        ins.then_inc(sems[inc], amt)
            return body
        block.sync(mk("sync"))
        block.gpsimd(mk("gpsimd"))
        block.tensor(mk("tensor"))
        block.vector(mk("vector"))
        block.scalar(mk("scalar"))


def mx(*toks):
    best = {}
    for t in toks:
        if t is None:
            continue
        for it in (t if isinstance(t, list) else [t]):
            if it is None:
                continue
            s, c = it
            if best.get(s, 0) < c:
                best[s] = c
    return list(best.items())


def build_program():
    nc = bass.Bass("TRN2", target_bir_lowering=False)

    dx = nc.dram_tensor("x", [T, D], F32, kind="ExternalInput").ap()
    dwq = nc.dram_tensor("wq", [8, 128, 4096], BF16, kind="ExternalInput").ap()
    dwk = nc.dram_tensor("wk", [8, 128, 4096], BF16, kind="ExternalInput").ap()
    dwv = nc.dram_tensor("wv", [8, 128, 4096], BF16, kind="ExternalInput").ap()
    dwo = nc.dram_tensor("wo", [8, 128, 4096], BF16, kind="ExternalInput").ap()
    dwgu = nc.dram_tensor("wgu", [HCN, 128, 4096], U8, kind="ExternalInput").ap()
    dwd = nc.dram_tensor("wd", [16, 128, HID], U8, kind="ExternalInput").ap()
    dsg = nc.dram_tensor("sg", [HCN, 128], F32, kind="ExternalInput").ap()
    dsu = nc.dram_tensor("su", [HCN, 128], F32, kind="ExternalInput").ap()
    dsd = nc.dram_tensor("sd", [16, 128], F32, kind="ExternalInput").ap()
    deye = nc.dram_tensor("eye", [128, 128], F32, kind="ExternalInput").ap()
    dmska = nc.dram_tensor("mska", [9, 128], BF16, kind="ExternalInput").ap()
    dmskb = nc.dram_tensor("mskb", [9, 512], BF16, kind="ExternalInput").ap()
    dy = nc.dram_tensor("y", [T, D], F32, kind="ExternalOutput").ap()

    BASE = 17408

    def at(name, shape, dt, o):
        return nc.alloc_sbuf_tensor_at(name, shape, dt, offset=BASE + o)

    xnT = at("xnT", [128, 16384], BF16, 0)
    out_nat = at("out_nat", [128, 8192], F32, 0)
    Qb = at("Qb", [128, 16384], BF16, 32768)
    Kb = at("Kb", [128, 16384], BF16, 65536)
    Vb = at("Vb", [128, 16384], BF16, 98304)
    actT = at("actT", [128, 45056], BF16, 32768)
    xstg0 = at("xstg0", [128, 2048], F32, 32768)
    xstg1 = at("xstg1", [128, 2048], F32, 40960)
    xn_bs = [at(f"xn_b{k}", [128, 2048], BF16, 49152 + 4096 * k) for k in range(4)]
    attnT = at("attnT", [128, 16384], BF16, 131072)
    A0 = 163840
    xbuf0 = at("xbuf0", [128, 2048], F32, A0)
    xbuf1 = at("xbuf1", [128, 2048], F32, A0 + 8192)
    e_f0 = at("e_f0", [128, 512], F32, A0)
    e_f1 = at("e_f1", [128, 512], F32, A0 + 2048)
    e_f2 = at("e_f2", [128, 512], F32, A0 + 4096)
    p_b0 = at("p_b0", [128, 512], BF16, A0 + 6144)
    p_b1 = at("p_b1", [128, 512], BF16, A0 + 7168)
    p_b2 = at("p_b2", [128, 512], BF16, A0 + 8192)
    pT_s0 = at("pT_s0", [128, 512], BF16, A0 + 9216)
    pT_s1 = at("pT_s1", [128, 512], BF16, A0 + 10240)
    pT_s2 = at("pT_s2", [128, 512], BF16, A0 + 11264)
    vT_s0 = at("vT_s0", [128, 512], BF16, A0 + 12288)
    vT_s1 = at("vT_s1", [128, 512], BF16, A0 + 13312)
    vT_s2 = at("vT_s2", [128, 512], BF16, A0 + 14336)
    gate_b = at("gate_b", [128, 1024], BF16, A0)
    up_b = at("up_b", [128, 1024], BF16, A0 + 2048)
    dwst0 = at("dwst0", [128, 2816], BF16, A0)
    dwst1 = at("dwst1", [128, 2816], BF16, A0 + 5632)
    outT0 = at("outT0", [128, 512], BF16, A0 + 14336)
    outT1 = at("outT1", [128, 512], BF16, A0 + 15360)
    wst0 = at("wst0", [128, 4096], BF16, 180224)
    wst1 = at("wst1", [128, 4096], BF16, 188416)
    m = 196608

    def ata(name, shape, dt, nbytes):
        nonlocal m
        m = (m + 63) // 64 * 64
        t = at(name, shape, dt, m)
        m += nbytes
        return t
    ata("pad0", [1, 32], F32, 128)
    eye_f = ata("eye_f", [128, 128], F32, 512)
    eye_b = ata("eye_b", [128, 128], BF16, 256)
    mkA = ata("mkA", [9, 128], BF16, 256)
    mkB = ata("mkB", [9, 512], BF16, 1024)
    shift_t = ata("shift_t", [128, 1], F32, 16)
    sgT = ata("sgT", [128, 44], F32, 192)
    suT = ata("suT", [128, 44], F32, 192)
    sdT = ata("sdT", [128, 16], F32, 64)
    eps_t = ata("eps_t", [128, 1], F32, 16)
    ones_c = ata("ones_c", [128, 1], BF16, 16)
    cneg = ata("cneg", [2, 128], BF16, 256)
    bn6 = ata("bn6", [128, 96], F32, 384)
    mv2 = ata("mv2", [128, 8], F32, 32)
    stdt = ata("stdt", [128, 4], F32, 16)
    rstd = ata("rstd", [128, 4], F32, 16)
    nmr = ata("nmr", [128, 4], F32, 16)
    rs4 = ata("rs4", [128, 12], F32, 48)
    ri4 = ata("ri4", [128, 12], F32, 48)
    hilo_xn = ata("hilo_xn", [2, 1024], BF16, 2048)
    hilo_ac = hilo_xn
    lo_b = ata("lo_b", [1, 512], BF16, 1024)
    scl_stage = ata("scl_stage", [44, 384], F32, 1536)
    corr_sb = [ata("corr0", [128, 512], F32, 2048), ata("corr1", [128, 512], F32, 2048)]
    assert BASE + m <= 229376 - 512, f"sbuf over budget: {BASE + m}"

    p0 = nc.alloc_psum_tensor("p0", [128, 512], F32)
    p1 = nc.alloc_psum_tensor("p1", [128, 512], F32)
    p2 = nc.alloc_psum_tensor("p2", [128, 512], F32)
    p3 = nc.alloc_psum_tensor("p3", [128, 512], F32)
    pb4 = nc.alloc_psum_tensor("pb4", [128, 1024], BF16)
    pb5 = nc.alloc_psum_tensor("pb5", [128, 1024], BF16)
    pb6 = nc.alloc_psum_tensor("pb6", [128, 1024], BF16)
    pb7 = nc.alloc_psum_tensor("pb7", [128, 1024], BF16)
    pf4 = pb4[:].bitcast(F32)
    pf5 = pb5[:].bitcast(F32)
    pf6 = pb6[:].bitcast(F32)
    pf7 = pb7[:].bitcast(F32)
    PBF = [p0, p1, p2, p3]
    ALLP = [p0, p1, p2, p3, pb4, pb5, pb6, pb7]

    P = Plan()
    for s in ("SV", "SS", "ST", "dxa", "dxb", "dsc", "dhl", "dn", "dout",
              "wa", "wb", "da", "db", "ca", "cb", "cc", "cd", "dsl", "dmk",
              "dxe", "dxf", "cg", "ch"):
        P.sem(s)

    bank_free = {id(b): None for b in ALLP}
    WST = [wst0, wst1]
    wslot_free = [None, None]
    wslot_sem = ["wa", "wb"]
    wct = {"i": 0}
    DWST = [dwst0, dwst1]
    dslot_free = [None, None]
    dslot_sem = ["da", "db"]
    dct = {"i": 0}
    XBUF = [xbuf0, xbuf1, xstg0, xstg1]
    xslot_free = [None, None, None, None]
    xslot_sem = ["dxa", "dxb", "ca", "cb"]
    xslot_sem2 = ["dxe", "dxf", "cg", "ch"]

    st_ = {
        "attn_pe_done": None,
        "attn_scr_free": None,
        "attnT_read": None,
        "actT_read": None,
        "out_store": None,
        "xnT_free": None,
        "ln_xbuf_done": None,
        "outT_free": [None, None],
        "corr_free": [None, None],
        "actT_read_early": None,
    }

    P.add("gpsimd", lambda e: e.dma_start(eye_f[:], deye), inc="dsc", amt=16)
    tok_eye = ("dsc", P.cnt["dsc"])
    P.add("gpsimd", lambda e: e.dma_start(scl_stage[0:44, 0:128], dsg), inc="dsl", amt=16)
    P.add("gpsimd", lambda e: e.dma_start(scl_stage[0:44, 128:256], dsu), inc="dsl", amt=16)
    P.add("gpsimd", lambda e: e.dma_start(scl_stage[0:16, 256:384], dsd), inc="dsl", amt=16)
    tok_scl = ("dsl", P.cnt["dsl"])
    P.add("gpsimd", lambda e: e.dma_start(mkA[:], dmska), inc="dmk", amt=16)
    P.add("gpsimd", lambda e: e.dma_start(mkB[:], dmskb), inc="dmk", amt=16)
    tok_mk = ("dmk", P.cnt["dmk"])
    tok_eyeb = P.add("vector", lambda e: e.tensor_copy(eye_b[:], eye_f[:]),
                     waits=mx(tok_eye), inc="SV")
    tok_eps = P.add("vector", lambda e: e.memset(eps_t[:], EPS), inc="SV")
    tok_ones = P.add("vector", lambda e: e.memset(ones_c[:], 1.0), inc="SV")
    tok_cneg = P.add("vector", lambda e: e.memset(cneg[:], -128.0), inc="SV")
    tok_shift = P.add("vector", lambda e: e.memset(shift_t[:], SHIFT), inc="SV")
    tk = P.add("tensor", lambda e: e.transpose(pf4[0:128, 0:44], scl_stage[0:44, 0:128], eye_f[0:44, 0:44]),
               waits=mx(tok_eye, tok_scl), inc="ST")
    tok_sg = P.add("scalar", lambda e: e.activation(sgT[:], pf4[0:128, 0:44], AF.Copy),
                   waits=mx(tk), inc="SS")
    tk = P.add("tensor", lambda e: e.transpose(pf4[0:128, 44:88], scl_stage[0:44, 128:256], eye_f[0:44, 0:44]),
               waits=mx(tok_sg), inc="ST")
    tok_su = P.add("scalar", lambda e: e.activation(suT[:], pf4[0:128, 44:88], AF.Copy),
                   waits=mx(tk), inc="SS")
    tk = P.add("tensor", lambda e: e.transpose(pf4[0:128, 88:104], scl_stage[0:16, 256:384], eye_f[0:16, 0:16]),
               waits=mx(tok_su), inc="ST")
    tok_sd = P.add("scalar", lambda e: e.activation(sdT[:], pf4[0:128, 88:104], AF.Copy),
                   waits=mx(tk), inc="SS")
    bank_free[id(pb4)] = tok_sd

    def stream_w(src_ap, waits_extra=()):
        i = wct["i"] % 2
        wct["i"] += 1
        dst = WST[i][:, 0:src_ap.shape[-1]]
        tokd = P.add("gpsimd", lambda e, d=dst, s=src_ap: e.dma_start(d, s),
                     waits=mx(wslot_free[i], *waits_extra), inc=wslot_sem[i], amt=16)
        return i, tokd

    def stream_d(src_ap, waits_extra=()):
        i = dct["i"] % 2
        dct["i"] += 1
        dst = DWST[i][:, 0:src_ap.shape[-1]]
        tokd = P.add("gpsimd", lambda e, d=dst, s=src_ap: e.dma_start(d, s),
                     waits=mx(dslot_free[i], *waits_extra), inc=dslot_sem[i], amt=16)
        return i, tokd

    def ln_phase(c):
        xnT_w = []
        prev_norm = [None, None, None, None]
        prev_aggr_rd = [None, None, None, None]
        prev_tr_last = [None, None, None, None]
        toks_x = [None] * 8

        def emit_x(i):
            xs = i % 4
            rows = c * CH + i * 128
            if xs < 2:
                w = mx(xslot_free[xs],
                       (dslot_free[0] if c > 0 else None),
                       (dslot_free[1] if c > 0 else None),
                       (st_["outT_free"][0] if c > 0 else None),
                       (st_["outT_free"][1] if c > 0 else None))
            else:
                w = mx(xslot_free[xs],
                       (st_["attn_pe_done"] if c > 0 else None),
                       (st_["actT_read_early"] if c > 0 else None))
            if i < 2:
                for q0, q1 in ((0, 32), (64, 96)):
                    P.add("sync",
                          lambda e, d=XBUF[xs][q0:q0 + 32, :], s=dx[rows + q0:rows + q0 + 32, :]:
                          e.dma_start(d, s), waits=w, inc=xslot_sem[xs], amt=16)
                    P.add("scalar",
                          lambda e, d=XBUF[xs][q1:q1 + 32, :], s=dx[rows + q1:rows + q1 + 32, :]:
                          e.dma_start(d, s), waits=w, inc=xslot_sem2[xs], amt=16)
                ta = (xslot_sem[xs], P.cnt[xslot_sem[xs]])
                tb = (xslot_sem2[xs], P.cnt[xslot_sem2[xs]])
            else:
                ta = P.add("sync",
                           lambda e, d=XBUF[xs][0:64, :], s=dx[rows:rows + 64, :]:
                           e.dma_start(d, s), waits=w, inc=xslot_sem[xs], amt=16)
                tb = P.add("scalar",
                           lambda e, d=XBUF[xs][64:128, :], s=dx[rows + 64:rows + 128, :]:
                           e.dma_start(d, s), waits=w, inc=xslot_sem2[xs], amt=16)
            toks_x[i] = (ta, tb)

        tnorm_all = [None] * 8
        PBQ = [pb4, pb5, pb6, pb7]

        def stage1(i):
            sl = i % 4
            xs = i % 4
            tb4 = None
            for j in range(4):
                tb4 = P.add("vector",
                            lambda e, o=bn6[:, 24 * sl + 6 * j:24 * sl + 6 * j + 6],
                                   s=XBUF[xs][:, 512 * j:512 * (j + 1)]: e.bn_stats(o, s),
                            waits=mx(toks_x[i][0], toks_x[i][1], prev_aggr_rd[sl]), inc="SV")
            tagg = P.add("vector",
                         lambda e, o=mv2[:, 2 * sl:2 * sl + 2], s=bn6[:, 24 * sl:24 * sl + 24]:
                         e.bn_aggr(o, s),
                         waits=mx(tb4, prev_norm[sl]), inc="SV")
            tsq = P.add("scalar",
                        lambda e, o=stdt[:, sl:sl + 1], s=mv2[:, 2 * sl + 1:2 * sl + 2]:
                        e.activation(o, s, AF.Sqrt, bias=eps_t[:]),
                        waits=mx(tagg, tok_eps, prev_norm[sl]), inc="SS")
            trc = P.add("vector",
                        lambda e, o=rstd[:, sl:sl + 1], s=stdt[:, sl:sl + 1]: e.reciprocal(o, s),
                        waits=mx(tsq, prev_norm[sl]), inc="SV")
            tnm = P.add("vector",
                        lambda e, o=nmr[:, sl:sl + 1], s=mv2[:, 2 * sl:2 * sl + 1],
                               r=rstd[:, sl:sl + 1]:
                        e.tensor_scalar(o, s, r, -1.0, ALU.mult, ALU.mult),
                        waits=mx(trc, prev_norm[sl]), inc="SV")
            xnb = xn_bs[sl]
            wnorm = mx(tagg, trc, tnm, prev_tr_last[sl],
                       st_["attn_pe_done"] if c > 0 else None,
                       st_["actT_read_early"] if c > 0 else None)
            tnorm_h = []
            for hh in range(2):
                tnorm = P.add("scalar",
                              lambda e, o=xnb[:, hh * 1024:(hh + 1) * 1024],
                                     s=XBUF[xs][:, hh * 1024:(hh + 1) * 1024],
                                     b=nmr[:, sl:sl + 1], r=rstd[:, sl:sl + 1]:
                              e.activation(o, s, AF.Identity, bias=b, scale=r),
                              waits=wnorm, inc="SS")
                tnorm_h.append(tnorm)
            tnorm_all[i] = tnorm_h
            prev_norm[sl] = tnorm_h[1]
            prev_aggr_rd[sl] = tnorm_h[1]
            xslot_free[xs] = tnorm_h[1]
            if i + 4 < 8:
                emit_x(i + 4)

        def stage2(i):
            sl = i % 4
            xnb = xn_bs[sl]
            tnorm_h = tnorm_all[i]
            last_tr = None
            for h in range(2):
                pb = PBQ[(i * 2 + h) % 4]
                t_tr = None
                for j in range(8):
                    kc = h * 8 + j
                    t_tr = P.add("tensor",
                                 lambda e, o=pb[:, j * 128:(j + 1) * 128],
                                        s=xnb[:, kc * 128:(kc + 1) * 128]:
                                 e.transpose(o, s, eye_b[:]),
                                 waits=mx(tnorm_h[h], tok_eyeb, bank_free[id(pb)]) if j == 0 else [],
                                 inc="ST" if j == 7 else None)
                dst = xnT[:].rearrange("p (k t) -> p k t", t=1024)[:, h * 8:(h + 1) * 8,
                                                                  i * 128:(i + 1) * 128]
                src = pb[:, 0:1024].rearrange("p (k t) -> p k t", t=128)
                wv = mx(t_tr, st_["out_store"] if c > 0 else None)
                if h == 0:
                    ev = P.add("scalar", lambda e, o=dst, s=src: e.activation(o, s, AF.Copy),
                               waits=wv, inc="SS")
                else:
                    ev = P.add("vector", lambda e, o=dst, s=src: e.tensor_copy(o, s),
                               waits=wv, inc="SV")
                bank_free[id(pb)] = ev
                xnT_w.append((i, ev))
                last_tr = t_tr
            prev_tr_last[sl] = last_tr

        for i in range(4):
            emit_x(i)
        for i in range(8):
            stage1(i)
            if i >= 1:
                stage2(i - 1)
        stage2(7)
        st_["ln_xbuf_done"] = mx(prev_norm[0], prev_norm[1])
        r0 = mx(*[t for (i, t) in xnT_w if i < 4])
        r1 = mx(*[t for (i, t) in xnT_w])
        return (r0, r1)

    def colsum(ready, dst_hilo, mov_fn, nacc, pbank, war):
        toks = []
        for tb in range(2):
            wt = mx(ready, bank_free[id(pbank)], tok_ones, war)
            t_mm = None
            for k in range(nacc):
                t_mm = P.add("tensor",
                             lambda e, o=pbank[0:1, 0:512], s=ones_c[:], r=mov_fn(k, tb),
                                    stt=(k == 0), sp=(k == nacc - 1):
                             e.matmul(o, s, r, start=stt, stop=sp),
                             waits=wt if k == 0 else [], inc="ST" if k == nacc - 1 else None)
            th = P.add("vector",
                       lambda e, o=dst_hilo[0:1, tb * 512:(tb + 1) * 512], s=pbank[0:1, 0:512]:
                       e.tensor_copy(o, s), waits=mx(t_mm, war, [("dhl", P.cnt["dhl"])]), inc="SV")
            tl = P.add("vector",
                       lambda e, o=lo_b[0:1, :], s0=pbank[0:1, 0:512],
                              s1=dst_hilo[0:1, tb * 512:(tb + 1) * 512]:
                       e.tensor_sub(o, s0, s1), waits=mx(th), inc="SV")
            bank_free[id(pbank)] = tl
            td = P.add("sync",
                       lambda e, o=dst_hilo[1:2, tb * 512:(tb + 1) * 512], s=lo_b[0:1, :]:
                       e.dma_start(o, s), waits=mx(tl), inc="dhl", amt=16)
            toks.append(td)
        return toks

    def qkv_phase(c, ready):
        evac_toks = []
        qk_toks = []
        for dw, dstb, qperm in ((dwq, Qb, True), (dwk, Kb, False), (dwv, Vb, False)):
            for pr in range(8):
                sl, tokd = stream_w(dw[pr])
                for b in range(2):
                    ob = pr * 2 + b
                    for tb in range(2):
                        bank = PBF[b * 2 + tb]
                        wt = mx(tokd, bank_free[id(bank)], ready[tb])
                        t_mm = None
                        for kc in range(KCN):
                            t_mm = P.add("tensor",
                                         lambda e, o=bank[:],
                                                s=WST[sl][:, b * 2048 + kc * 128: b * 2048 + (kc + 1) * 128],
                                                r=xnT[:, kc * 1024 + tb * 512: kc * 1024 + tb * 512 + 512],
                                                stt=(kc == 0), sp=(kc == KCN - 1):
                                         e.matmul(o, s, r, start=stt, stop=sp),
                                         waits=wt if kc == 0 else [], inc="ST" if kc == KCN - 1 else None)
                        wslot_free[sl] = t_mm
                        slot_h = PERM[ob] if qperm else ob
                        dview = dstb[:].rearrange("p (g s i) -> p g s i", s=NH, i=8)
                        dst = dview[:, tb * 64:(tb + 1) * 64, slot_h, :]
                        src = bank[:].rearrange("p (g i) -> p g i", i=8)
                        wv = mx(t_mm,
                                st_["attn_pe_done"] if c > 0 else None,
                                st_["actT_read"] if c > 0 else None)
                        if (ob + tb) % 2 == 0:
                            ev = P.add("scalar", lambda e, o=dst, s=src: e.activation(o, s, AF.Copy),
                                       waits=wv, inc="SS")
                        else:
                            ev = P.add("vector", lambda e, o=dst, s=src: e.tensor_copy(o, s),
                                       waits=wv, inc="SV")
                        bank_free[id(bank)] = ev
                        evac_toks.append(ev)
                        if dstb is not Vb:
                            qk_toks.append(ev)
        return mx(*qk_toks), mx(*evac_toks)

    def attn_phase(c, qk_done, qkv_done):
        EF = [e_f0, e_f1, e_f2]
        PBM = [p_b0, p_b1, p_b2]
        PTS = [pT_s0, pT_s1, pT_s2]
        VTS = [vT_s0, vT_s1, vT_s2]
        PSC = [p0, p1]
        PAV = [p2, p3]
        prev = {}
        ev_toks = []
        state = {"last_pe": None, "t_e_last": None, "t_mu_last": None}
        pendA = []
        pendB = []

        def do_tailA(bb, t_pm_toks):
            par = bb % 2
            par3 = bb % 3
            pPT = pb4 if par == 0 else pb5
            t_tr = None
            for j in range(4):
                t_tr = P.add("tensor",
                             lambda e, o=pPT[:, j * 128:(j + 1) * 128],
                                    s=PBM[par3][:, j * 128:(j + 1) * 128]:
                             e.transpose(o, s, eye_b[:]),
                             waits=mx(t_pm_toks[j],
                                      bank_free[id(pPT)] if j == 0 else None,
                                      tok_eyeb if j == 0 else None),
                             inc="ST" if j == 3 else None)
            prev[("pbm", par3)] = t_tr
            t_pte = P.add("scalar", lambda e, o=PTS[par3][:], s=pPT[:, 0:512]: e.activation(o, s, AF.Copy),
                          waits=mx(t_tr, prev.get(("pts", par3))), inc="SS")
            bank_free[id(pPT)] = t_pte
            pendB.append((bb, t_pte))

        def do_tailB(bb, t_pte):
            par3 = bb % 3
            pAV = PAV[bb % 2]
            t_av = None
            for j in range(4):
                t_av = P.add("tensor",
                             lambda e, o=pAV[:, j * 128:(j + 1) * 128],
                                    s=VTS[par3][:, j * 128:(j + 1) * 128],
                                    r=PTS[par3][:, j * 128:(j + 1) * 128]:
                             e.matmul(o, s, r, start=True, stop=True),
                             waits=mx(prev.get(("vte", par3)), t_pte, bank_free[id(pAV)]) if j == 0 else [],
                             inc="ST" if j == 3 else None)
            prev[("vts", par3)] = t_av
            prev[("pts", par3)] = t_av
            state["last_pe"] = t_av
            dst = attnT[:].rearrange("p (h gg i) -> p h gg i", gg=NGRP, i=8)[:, :, bb * 4:(bb + 1) * 4, :]
            src = pAV[:].rearrange("p (g h i) -> p h g i", g=4, i=8)
            t_ev = P.add("scalar", lambda e, o=dst, s=src: e.activation(o, s, AF.Copy),
                         waits=mx(t_av, prev.get(("avev",)),
                                  st_["attnT_read"] if c > 0 else None), inc="SS")
            prev[("avev",)] = t_ev
            bank_free[id(pAV)] = t_ev
            ev_toks.append(t_ev)

        for b in range(NBAT):
            par = b % 2
            par3 = b % 3
            pS = PSC[b % 2]
            pVT = pb6 if par == 0 else pb7
            for j in range(4):
                g = b * 4 + j
                P.add("tensor",
                      lambda e, o=pS[:, j * 128:(j + 1) * 128],
                             s=Qb[:, g * 128:(g + 1) * 128], r=Kb[:, g * 128:(g + 1) * 128],
                             stt=(j == 0):
                      e.matmul(o, s, r, start=stt, stop=False),
                      waits=mx(qk_done, bank_free[id(pS)]) if j == 0 else [],
                      inc=None)
            t_sc = P.add("tensor",
                         lambda e, o=pS[:], s=mkA[:], r=mkB[:]:
                         e.matmul(o, s, r, start=False, stop=True),
                         waits=mx(tok_mk), inc="ST")
            t_vt = None
            for j in range(4):
                g = b * 4 + j
                t_vt = P.add("tensor",
                             lambda e, o=pVT[:, j * 128:(j + 1) * 128], s=Vb[:, g * 128:(g + 1) * 128]:
                             e.transpose(o, s, eye_b[:]),
                             waits=mx(qkv_done, bank_free[id(pVT)], tok_eyeb) if j == 0 else [],
                             inc="ST" if j == 3 else None)
            if len(pendA) == 2:
                do_tailA(*pendA.pop(0))
            if len(pendB) == 2:
                do_tailB(*pendB.pop(0))
            if b % 2 == 0:
                t_vte = P.add("vector", lambda e, o=VTS[par3][:], s=pVT[:, 0:512]: e.tensor_copy(o, s),
                              waits=mx(t_vt, prev.get(("vts", par3))), inc="SV")
            else:
                t_vte = P.add("scalar", lambda e, o=VTS[par3][:], s=pVT[:, 0:512]: e.activation(o, s, AF.Copy),
                              waits=mx(t_vt, prev.get(("vts", par3))), inc="SS")
            prev[("vte", par3)] = t_vte
            bank_free[id(pVT)] = t_vte
            t_e = P.add("scalar",
                        lambda e, o=EF[par3][:], s=pS[:]:
                        e.activation(o, s, AF.Exp, bias=shift_t[:], scale=RSQ),
                        waits=mx(t_sc, tok_shift, prev.get(("ef", par3)),
                                 st_["ln_xbuf_done"]), inc="SS")
            bank_free[id(pS)] = t_e
            state["t_e_last"] = t_e
            rsl = rs4[:, par3 * 4:(par3 + 1) * 4]
            ril = ri4[:, par3 * 4:(par3 + 1) * 4]
            t_rs = P.add("vector",
                         lambda e, o=rsl,
                                s=EF[par3][:].rearrange("p (g cc) -> p g cc", g=4):
                         e.reduce_sum(o, s, axis=AX.X),
                         waits=mx(t_e, prev.get(("rs", par3))), inc="SV")
            t_ri = P.add("vector", lambda e, o=ril, s=rsl: e.reciprocal(o, s),
                         waits=mx(t_rs, prev.get(("ri", par3))), inc="SV")
            prev[("rs", par3)] = t_ri
            t_pms = []
            for j in range(4):
                t_pm = P.add("vector",
                             lambda e, o=PBM[par3][:, j * 128:(j + 1) * 128],
                                    s=EF[par3][:, j * 128:(j + 1) * 128],
                                    r=ril[:, j:j + 1]:
                             e.tensor_scalar(o, s, r, None, ALU.mult),
                             waits=mx(t_ri, prev.get(("pbm", par3))) if j == 0 else [], inc="SV")
                t_pms.append(t_pm)
            state["t_mu_last"] = t_pms[-1]
            prev[("ef", par3)] = t_pms[-1]
            prev[("ri", par3)] = t_pms[-1]
            pendA.append((b, t_pms))
        while pendA or pendB:
            if pendB:
                do_tailB(*pendB.pop(0))
            if pendA:
                do_tailA(*pendA.pop(0))
        st_["attn_pe_done"] = state["last_pe"]
        st_["attn_scr_free"] = mx(state["t_e_last"], state["t_mu_last"], state["last_pe"])
        return mx(*ev_toks)

    def ffn_phase(c, ready, attn_done):
        mult_toks = []
        prev_mu = [None, None]
        last_mm = None
        for hb in range(HCN):
            sl, tokd = stream_w(dwgu[hb])
            for tb in range(2):
                gbank = PBF[tb]
                ubank = PBF[2 + tb]
                toks_mm = {}
                for which, bank, coff in (("g", gbank, 0), ("u", ubank, 2048)):
                    wt = mx(tokd, bank_free[id(bank)], ready[tb])
                    t_mm = None
                    for kc in range(KCN):
                        t_mm = P.add("tensor",
                                     lambda e, o=bank[:],
                                            s=WST[sl][:, coff + kc * 128: coff + (kc + 1) * 128],
                                            r=xnT[:, kc * 1024 + tb * 512: kc * 1024 + tb * 512 + 512],
                                            stt=(kc == 0), sp=(kc == KCN - 1):
                                     e.matmul(o, s, r, start=stt, stop=sp),
                                     waits=wt if kc == 0 else [], inc="ST" if kc == KCN - 1 else None)
                    toks_mm[which] = t_mm
                    last_mm = t_mm
                wslot_free[sl] = toks_mm["u"]
                gb = gate_b[:, tb * 512:(tb + 1) * 512]
                ub = up_b[:, tb * 512:(tb + 1) * 512]
                t_si = P.add("scalar",
                             lambda e, o=gb, s=gbank[:], sc=sgT[:, hb:hb + 1]:
                             e.activation(o, s, AF.Silu, scale=sc),
                             waits=mx(toks_mm["g"], tok_sg, prev_mu[tb],
                                      st_["attn_scr_free"]), inc="SS")
                bank_free[id(gbank)] = t_si
                t_ue = P.add("scalar",
                             lambda e, o=ub, s=ubank[:], sc=suT[:, hb:hb + 1]:
                             e.activation(o, s, AF.Copy, scale=sc),
                             waits=mx(toks_mm["u"], tok_su, prev_mu[tb],
                                      st_["attn_scr_free"]), inc="SS")
                bank_free[id(ubank)] = t_ue
                dst = actT[:, hb * 1024 + tb * 512: hb * 1024 + tb * 512 + 512]
                t_mu = P.add("vector", lambda e, o=dst, s0=gb, s1=ub: e.tensor_mul(o, s0, s1),
                             waits=mx(t_si, t_ue, st_["attn_pe_done"], attn_done,
                                      st_["actT_read"] if c > 0 else None), inc="SV")
                prev_mu[tb] = t_mu
                mult_toks.append(t_mu)
        st_["xnT_free"] = last_mm
        return mx(*mult_toks)

    def out_phase(c, act_done, attn_done):
        hilo2 = colsum(act_done, hilo_ac,
                       lambda k, tb: actT[:, k * 1024 + tb * 512: k * 1024 + tb * 512 + 512],
                       HCN, p0,
                       war=mx(st_["actT_read"] if c > 0 else None))
        last_wo = None
        last_dn = None
        corr_toks = [None, None]
        for tb in range(2):
            pcb = [pb6, pb7][tb]
            pcf = [pf6, pf7][tb]
            t_cm = P.add("tensor",
                         lambda e, o=pcf[0:128, 0:512], s=cneg[:], r=hilo_ac[:, tb * 512:(tb + 1) * 512]:
                         e.matmul(o, s, r, start=True, stop=True),
                         waits=mx(hilo2, tok_cneg, bank_free[id(pcb)]), inc="ST")
            t_ce = P.add("scalar",
                         lambda e, o=corr_sb[tb][:], s=pcf[0:128, 0:512]: e.activation(o, s, AF.Copy),
                         waits=mx(t_cm, st_["corr_free"][tb]), inc="SS")
            bank_free[id(pcb)] = t_ce
            corr_toks[tb] = t_ce
        for tb in range(2):
            rows = c * CH + tb * 512
            dnv = out_nat[:].rearrange("p (j cc) -> p j cc", j=4)
            srcx = dx[rows:rows + 512, :].rearrange("(j p) cc -> p j cc", p=128)
            t_pre = P.add("sync", lambda e, o=dnv, s=srcx: e.dma_start(o, s),
                          waits=mx(act_done, st_["xnT_free"], st_["out_store"]),
                          inc="dn", amt=16)
            add_toks = []
            prev_add = [None]
            pend = [None]

            def do_tail2(ot, tp, pbank, t_st, oc):
                t_tr = None
                for j in range(4):
                    t_tr = P.add("tensor",
                                 lambda e, o=tp[:, j * 128:(j + 1) * 128],
                                        s=ot[:, j * 128:(j + 1) * 128]:
                                 e.transpose(o, s, eye_b[:]),
                                 waits=mx(t_st, bank_free[id(pbank)]) if j == 0 else [],
                                 inc="ST" if j == 3 else None)
                st_["outT_free"][oc % 2] = t_tr
                dnv3 = out_nat[:].rearrange("p (j cc) -> p j cc", j=4)[:, :, oc * 128:(oc + 1) * 128]
                pv3 = tp[:, 0:512].rearrange("p (j cc) -> p j cc", j=4)
                t_ad = P.add("vector",
                             lambda e, o=dnv3, s0=pv3, s1=dnv3: e.tensor_tensor(o, s0, s1, ALU.add),
                             waits=mx(t_tr, t_pre, prev_add[0]), inc="SV")
                bank_free[id(pbank)] = t_ad
                prev_add[0] = t_ad
                add_toks.append(t_ad)

            for oc in range(16):
                if oc % 2 == 0:
                    wo_sl, wo_tok = stream_w(dwo[oc // 2])
                d_sl0, d_tok0 = stream_d(dwd[oc][:, 0:2816],
                                         waits_extra=(st_["ln_xbuf_done"], st_["attn_scr_free"],
                                                      act_done))
                d_sl1, d_tok1 = stream_d(dwd[oc][:, 2816:5632],
                                         waits_extra=(st_["ln_xbuf_done"], st_["attn_scr_free"],
                                                      act_done))
                b = oc % 2
                wobank = PBF[b]
                dbank = PBF[2 + b]
                wt = mx(wo_tok, bank_free[id(wobank)], attn_done)
                t_mm = None
                for kc in range(KCN):
                    t_mm = P.add("tensor",
                                 lambda e, o=wobank[:],
                                        s=WST[wo_sl][:, b * 2048 + kc * 128: b * 2048 + (kc + 1) * 128],
                                        r=attnT[:, kc * 1024 + tb * 512: kc * 1024 + tb * 512 + 512],
                                        stt=(kc == 0), sp=(kc == KCN - 1):
                                 e.matmul(o, s, r, start=stt, stop=sp),
                                 waits=wt if kc == 0 else [], inc="ST" if kc == KCN - 1 else None)
                last_wo = t_mm
                if b == 1:
                    wslot_free[wo_sl] = t_mm
                wt = mx(d_tok0, bank_free[id(dbank)], act_done)
                wt22 = mx(d_tok1)
                t_dmm = None
                for hc in range(HCN):
                    piece_sl = d_sl0 if hc < 22 else d_sl1
                    cbase = (hc % 22) * 128
                    inc = "ST" if hc in (21, HCN - 1) else None
                    t_dmm = P.add("tensor",
                                  lambda e, o=dbank[:], s=DWST[piece_sl][:, cbase:cbase + 128],
                                         r=actT[:, hc * 1024 + tb * 512: hc * 1024 + tb * 512 + 512],
                                         stt=(hc == 0), sp=(hc == HCN - 1):
                                  e.matmul(o, s, r, start=stt, stop=sp),
                                  waits=(wt if hc == 0 else (wt22 if hc == 22 else [])), inc=inc)
                    if hc == 21:
                        dslot_free[d_sl0] = t_dmm
                        if tb == 1 and oc == 15:
                            st_["actT_read_early"] = t_dmm
                    if hc == HCN - 1:
                        dslot_free[d_sl1] = t_dmm
                last_dn = t_dmm
                ot = outT0 if oc % 2 == 0 else outT1
                t_cp = P.add("vector",
                             lambda e, o=ot[:], s0=corr_sb[tb][:], sc=sdT[:, oc:oc + 1],
                                    s1=wobank[:]:
                             e.scalar_tensor_tensor(o, s0, sc, s1, ALU.mult, ALU.add),
                             waits=mx(t_mm, corr_toks[tb], tok_sd,
                                      st_["outT_free"][oc % 2], st_["ln_xbuf_done"],
                                      st_["attn_scr_free"]), inc="SV")
                bank_free[id(wobank)] = t_cp
                t_st = P.add("vector",
                             lambda e, o=ot[:], s0=dbank[:], sc=sdT[:, oc:oc + 1], s1=ot[:]:
                             e.scalar_tensor_tensor(o, s0, sc, s1, ALU.mult, ALU.add),
                             waits=mx(t_dmm, t_cp, tok_sd), inc="SV")
                bank_free[id(dbank)] = t_st
                if pend[0] is not None:
                    do_tail2(*pend[0])
                pend[0] = (ot, pb4 if oc % 2 == 0 else pb5,
                           pb4 if oc % 2 == 0 else pb5, t_st, oc)
            do_tail2(*pend[0])
            st_["corr_free"][tb] = t_cp
            wA = mx(*add_toks[:8])
            wB = mx(*add_toks)
            for j in range(4):
                qa = ("sync", "scalar", "sync", "scalar")[j]
                P.add(qa,
                      lambda e, o=dy[rows + j * 128:rows + (j + 1) * 128, 0:1024],
                             s=out_nat[:, j * 2048:j * 2048 + 1024]:
                      e.dma_start(o, s),
                      waits=wA, inc="dout", amt=16)
            for j in range(4):
                qb = ("scalar", "sync", "scalar", "sync")[j]
                P.add(qb,
                      lambda e, o=dy[rows + j * 128:rows + (j + 1) * 128, 1024:2048],
                             s=out_nat[:, j * 2048 + 1024:(j + 1) * 2048]:
                      e.dma_start(o, s),
                      waits=wB, inc="dout", amt=16)
            st_["out_store"] = ("dout", P.cnt["dout"])
        st_["attnT_read"] = last_wo
        st_["actT_read"] = last_dn

    for c in range(NCH):
        ready = ln_phase(c)
        qk_done, qkv_done = qkv_phase(c, ready)
        attn_done = attn_phase(c, qk_done, qkv_done)
        act_done = ffn_phase(c, ready, attn_done)
        out_phase(c, act_done, attn_done)

    P.add("sync", lambda e: e.nop(), waits=[("dout", P.cnt["dout"])])

    import contextlib
    with contextlib.ExitStack() as stx:
        sems = {}
        for s in P.cnt:
            sems[s] = stx.enter_context(nc.semaphore(s))
        block = stx.enter_context(nc.Block())
        P.emit(block, sems)
    return nc


_NC_CACHE = {}


def _prep_weights(Wq, Wk, Wv, Wo, wq_gate, wq_up, wq_down):
    def qkv_tile(W):
        WT = np.ascontiguousarray(W.T)
        t = WT.reshape(16, 128, 16, 128).transpose(2, 1, 0, 3).reshape(16, 128, 2048)
        t = np.ascontiguousarray(
            t.reshape(8, 2, 128, 2048).transpose(0, 2, 1, 3).reshape(8, 128, 4096))
        return np.ascontiguousarray(
            t.view(np.uint8).reshape(8, 128, 4096, 4)[..., 2:4]).view(ml_dtypes.bfloat16).reshape(8, 128, 4096)

    def gu_tile(W):
        WT = np.ascontiguousarray(W.T)
        return WT.reshape(16, 128, 44, 128).transpose(2, 1, 0, 3).reshape(44, 128, 2048)

    wq = qkv_tile(Wq)
    wk = qkv_tile(Wk)
    wv = qkv_tile(Wv)
    wo = qkv_tile(Wo)
    wgu = np.ascontiguousarray(np.concatenate([gu_tile(wq_gate), gu_tile(wq_up)], axis=2))
    WdT = np.ascontiguousarray(wq_down.T)
    wd = np.ascontiguousarray(
        WdT.reshape(44, 128, 16, 128).transpose(2, 1, 0, 3).reshape(16, 128, 5632))
    wgu = np.ascontiguousarray(wgu.view(np.uint8).reshape(44, 128, 4096, 4)[..., 0])
    wd = np.ascontiguousarray(wd.view(np.uint8).reshape(16, 128, 5632, 4)[..., 0])
    return wq, wk, wv, wo, wgu, wd


def kernel(L0_attn_x, L0_attn_Wq, L0_attn_Wk, L0_attn_Wv, L0_attn_Wo,
           L0_ffn_wq_gate, L0_ffn_s_gate, L0_ffn_wq_up, L0_ffn_s_up,
           L0_ffn_wq_down, L0_ffn_s_down):
    x = np.ascontiguousarray(np.asarray(L0_attn_x, dtype=np.float32).reshape(16384, 2048))
    wq, wk, wv, wo, wgu, wd = _prep_weights(
        np.asarray(L0_attn_Wq, np.float32), np.asarray(L0_attn_Wk, np.float32),
        np.asarray(L0_attn_Wv, np.float32), np.asarray(L0_attn_Wo, np.float32),
        np.asarray(L0_ffn_wq_gate, np.int32), np.asarray(L0_ffn_wq_up, np.int32),
        np.asarray(L0_ffn_wq_down, np.int32))
    sg = np.ascontiguousarray(np.asarray(L0_ffn_s_gate, np.float32).reshape(44, 128))
    su = np.ascontiguousarray(np.asarray(L0_ffn_s_up, np.float32).reshape(44, 128))
    sd = np.ascontiguousarray(np.asarray(L0_ffn_s_down, np.float32).reshape(16, 128))
    eye = np.eye(128, dtype=np.float32)
    idx = np.arange(128)
    C = float(2 ** 43)
    mska = np.ones((9, 128), dtype=ml_dtypes.bfloat16)
    mskb = np.full((9, 512), -C, dtype=ml_dtypes.bfloat16)
    for r in range(8):
        mska[1 + r] = (idx % 8 == r).astype(ml_dtypes.bfloat16)
        mskb[1 + r] = np.tile(((idx % 8 == r) * C).astype(ml_dtypes.bfloat16), 4)
    mskb[0] = -C

    if "nc" not in _NC_CACHE:
        _NC_CACHE["nc"] = build_program()
    nc = _NC_CACHE["nc"]

    in_maps = []
    for ci in range(8):
        shard = np.ascontiguousarray(x[ci * 2048:(ci + 1) * 2048])
        in_maps.append(dict(x=shard, wq=wq, wk=wk, wv=wv, wo=wo, wgu=wgu, wd=wd,
                            sg=sg, su=su, sd=sd, eye=eye, mska=mska, mskb=mskb))
    res = run_bass_kernel_spmd(nc, in_maps, core_ids=list(range(8)))
    y = np.concatenate([np.asarray(res.results[i]["y"]) for i in range(8)], axis=0)
    return y.reshape(2, 8192, 2048).astype(np.float32)
